# version 09
# baseline (speedup 1.0000x reference)
from contextlib import ExitStack

import numpy as np
import ml_dtypes

import concourse.bass as bass
import concourse.mybir as mybir
from concourse.bass_utils import run_bass_kernel_spmd

F32 = mybir.dt.float32
BF16 = mybir.dt.bfloat16
I32 = mybir.dt.int32
AF = mybir.ActivationFunctionType
ALU = mybir.AluOpType
AX = mybir.AxisListType
NBF = ml_dtypes.bfloat16

NCORES = 8
BC = 4
L = 2048
D = 1024
DI = 2048
DH = 4096
NS = 16
T = 512
NTS = L // T
NTILES = BC * NTS
NSUB = T // 128
DC = D // 128
DIC = DI // 128
DHC = DH // 128
TX = T + 1

DEBUG = False
AUDIT = True
AUDIT_ENGINES = {"dv"}
AUDIT_CUTOFF = None
AUDIT_INC_ONLY = False
AUDIT_DEPTH = 1
AUDIT_PHASES = None
DRAIN_MARKS = False
RACE_DETECT = False
PHASE = ["startup"]

BIGW = [("fc1", D, D), ("inp", 2 * DI, D), ("op", D, DI), ("fc2", D, D),
        ("w1", DH, D), ("w2", D, DH)]


class Eng:
    def __init__(self, sem, name="", drain=False):
        self.sem = sem
        self.name = name
        self.drain = drain
        self.ops = []
        self.n = 0

    def _audit(self):
        if not (AUDIT and (AUDIT_ENGINES is None or self.name in AUDIT_ENGINES)):
            return False
        if AUDIT_PHASES is not None and PHASE[0] not in AUDIT_PHASES:
            return False
        return AUDIT_CUTOFF is None or len(self.ops) < AUDIT_CUTOFF

    def do(self, fn):
        if self._audit():
            self.mark(fn)
        else:
            self.ops.append(fn)

    def mark(self, fn, k=1, sem=None):
        sem = self.sem if sem is None else sem
        if self._audit() and sem is self.sem and not AUDIT_INC_ONLY:
            prior = self.n - (AUDIT_DEPTH - 1)
            self.ops.append(lambda q, s=self.sem, c=prior: q.wait_ge(s, c)
                            if c > 0 else None)
        if self._audit() and sem is not self.sem:
            self.ops.append(
                lambda q, fn=fn, sem=sem, k=k, own=self.sem, c=self.n:
                (q.wait_ge(own, c) if c > 0 else None,
                 fn(q).then_inc(sem, k))[-1])
            return None
        if self.drain:
            self.ops.append(lambda q, fn=fn: fn(q))
            self.ops.append(
                lambda q, sem=sem, k=k: q.drain().then_inc(sem, k))
        else:
            self.ops.append(
                lambda q, fn=fn, sem=sem, k=k: fn(q).then_inc(sem, k))
        if sem is self.sem:
            self.n += k
            return self.n
        return None

    def wait_sem(self, sem, cnt):
        if cnt > 0:
            self.ops.append(lambda q, s=sem, c=cnt: q.wait_ge(s, c))

    def wait(self, other, cnt):
        if cnt > 0:
            self.ops.append(lambda q, s=other.sem, c=cnt: q.wait_ge(s, c))

    def self_wait(self):
        self.ops.append(lambda q, s=self.sem, c=self.n: q.wait_ge(s, c))


def build_program():
    nc = bass.Bass("TRN2", target_bir_lowering=False,
                   detect_race_conditions=RACE_DETECT)
    dram = {}

    def din(name, shape, dt=F32):
        dram[name] = nc.dram_tensor(name, shape, dt, kind="ExternalInput").ap()
        return dram[name]

    x_in = din("x", [BC, L, D])
    for nm, M, K in BIGW:
        din("h_" + nm, [M // 128, 128, K])
    cw1 = din("cw1", [DI, 4])
    cb1 = din("cb1", [1, DI])
    xpT = din("xpT", [DI, 96])
    dtpT = din("dtpT", [64, DI])
    dtb = din("dtb", [1, DI])
    Alog = din("Alog", [DI, NS])
    D_in = din("Dvec", [1, DI])
    cw2 = din("cw2", [DH, 3])
    cb2 = din("cb2", [1, DH])
    fc1b = din("fc1b", [1, D])
    fc2b = din("fc2b", [1, D])
    idn_bf = din("idn_bf", [128, 128], BF16)
    idn_f32 = din("idn_f32", [128, 128])
    ones_row_bf = din("ones_row_bf", [1, 128], BF16)
    ones_row_f32 = din("ones_row_f32", [1, 128])
    zeros_in = din("zeros_in", [128, 256])
    cst2_in = din("cst2_in", [128, 2])

    out_d = nc.dram_tensor("out", [BC, L, D], F32, kind="ExternalOutput").ap()
    scr = {}
    for nm, M, K in BIGW:
        scr[nm] = nc.dram_tensor("s_" + nm, [M // 128, 128, K], BF16,
                                 kind="Internal").ap()
    scr["cv1"] = nc.dram_tensor("s_cv1", [DIC, 128, 4 * 128], BF16,
                                kind="Internal").ap()
    scr["cv2"] = nc.dram_tensor("s_cv2", [DHC, 128, 3 * 128], BF16,
                                kind="Internal").ap()
    s_h2 = nc.dram_tensor("s_h2", [DHC, 128, T], BF16, kind="Internal").ap()

    with ExitStack() as ctx:
        e = ctx.enter_context

        def sb(name, shape, dt=BF16):
            return e(nc.sbuf_tensor(name, shape, dt))

        x_tm = sb("x_tm", [128, NSUB * D], F32)[:]
        PA = sb("PA", [128, 2 * DIC * T])[:]
        xq_tm = PA[:, 0:(NSUB + 1) * D]
        u_sl = PA[:, 0:DIC * T]
        sz_sl = PA[:, DIC * T:2 * DIC * T]
        PB = sb("PB", [128, DHC * T])[:]
        xn_tm = PB[:, NS * T:NS * T + NSUB * D]
        Bb_sl = PB[:, 0:NS * T]
        Cb_sl = PB[:, NS * T:2 * NS * T]
        bc0 = PB[0:1, 0:2 * NS * T]
        h2_sl = PB
        PH = sb("PH", [128, 2 * DC * T])[:]
        xn_fm = PH[:, 0:DC * T]
        hfc1 = PH[:, DC * T:2 * DC * T]
        h_sl = PH[:, 0:NS * T]
        attnm = PH.bitcast(BF16)[:, 0:DC * T]
        PY = sb("PY", [128, DC * T], F32)[:]
        y_sl = PY.bitcast(BF16)[:, 0:DIC * T]
        attn32 = PY
        PF = sb("PF", [128, 4 * D], F32)[:]
        xn32 = PF[:, 0:D]
        xsq = PF[:, D:2 * D]
        qpre_tm = PF[:, 2 * D:3 * D]
        qi_tm = PF[:, 3 * D:4 * D].bitcast(I32)
        ffn32 = PF
        PC = sb("PC", [128, NSUB * D], F32)[:]
        a_ring = PC[:, 0:4 * T]
        dtr_r = PC[:, 4 * T:6 * T]
        esp_r = PC[:, 6 * T:8 * T]
        out_tm = PC
        h1r = sb("h1r", [128, 4 * (T + 2)])[:]
        h2r = sb("h2r", [128, 3 * T])[:]
        mx_t = sb("mx_t", [128, T])[:]
        mn_t = sb("mn_t", [128, T])[:]
        PD = sb("PD", [128, 8 * T])[:]
        b_ring = PD[:, 0:4 * T]
        dt_r = PD[:, 4 * T:6 * T]
        dtu_r = PD[:, 6 * T:8 * T]
        qpre_r = PD[:, 0:4 * T].bitcast(F32)
        qi_r = PD[:, 4 * T:8 * T].bitcast(I32)
        srow = PD[0:1, 0:5 * 128 * 2].bitcast(F32)
        srow_bf = PD[0:1, 5 * 128 * 2:5 * 128 * 3]
        xq_fm = sb("xq_fm", [128, DC * TX])[:]
        cacc = sb("cacc", [128, 2 * T], F32)[:]
        xdbl = cacc[0:96, 0:T]
        x_extra = cacc[0:1, 0:D]
        dtlow = cacc[:, T:T + T // 2].bitcast(BF16)[0:64, :]
        xc_ring = sb("xc_ring", [128, 2 * (T + 3)])[:]
        xc_tail = sb("xc_tail", [128, DIC * 3])[:]
        st_sl = sb("st_sl", [128, DIC * NS], F32)[:]
        h1_tail = sb("h1_tail", [128, DHC])[:]
        colA = sb("colA", [128, 8], F32)[:]
        colB = sb("colB", [128, 8], F32)[:]
        colC = sb("colC", [128, 8], F32)[:]
        m1c = sb("m1c", [128, NSUB + 1], F32)[:]
        s1c = sb("s1c", [128, NSUB + 1], F32)[:]
        f1c = sb("f1c", [128, NSUB + 1], F32)[:]
        m2c = sb("m2c", [128, NSUB], F32)[:]
        s2c = sb("s2c", [128, NSUB], F32)[:]
        f2c = sb("f2c", [128, NSUB], F32)[:]
        s2b = sb("s2b", [128, T])[:]
        f1b = sb("f1b", [128, T + 1])[:]
        f2b = sb("f2b", [128, T])[:]
        NSLOT = 2
        wring = sb("wring", [128, NSLOT * 4096])[:]
        A_sb = sb("A_sb", [128, DIC * NS], F32)[:]
        D_sb = sb("D_sb", [128, DIC], F32)[:]
        D_sbf = sb("D_sbf", [128, DIC])[:]
        cw1_sb = sb("cw1_sb", [128, DIC * 4], F32)[:]
        cb1_sb = sb("cb1_sb", [128, DIC], F32)[:]
        cw2_sb = sb("cw2_sb", [128, DHC * 3], F32)[:]
        cb2_sb = sb("cb2_sb", [128, DHC], F32)[:]
        fc1b_sb = sb("fc1b_sb", [128, DC], F32)[:]
        fc2b_sb = sb("fc2b_sb", [128, DC], F32)[:]
        dtb_sb = sb("dtb_sb", [128, DIC], F32)[:]
        xpT_sb = sb("xpT_sb", [128, DIC * 96])[:]
        dtpT_sb = e(nc.sbuf_tensor("dtpT_sb", [64, DI], BF16))[:]
        idnb = sb("idnb", [128, 128])[:]
        idnf = sb("idnf", [128, 128], F32)[:]
        onesb = sb("onesb", [1, 128])[:]
        onesf = sb("onesf", [1, 128], F32)[:]
        zeros = sb("zeros", [128, 256], F32)[:]
        zbf = zeros.bitcast(BF16)
        cst2 = sb("cst2", [128, 2], F32)[:]
        epsc = cst2[:, 0:1]
        onec = cst2[:, 1:2]
        cq1 = sb("cq1", [128, 1], F32)[:]
        cq2 = sb("cq2", [128, 1], F32)[:]
        swcol = sb("swcol", [128, 1], F32)[:]
        tcol = sb("tcol", [1, 4], F32)[:]
        pA = e(nc.psum_tensor("pA", [128, T], F32))
        pB = e(nc.psum_tensor("pB", [128, T], F32))
        pX = e(nc.psum_tensor("pX", [128, T], F32))
        pC = e(nc.psum_tensor("pC", [128, T], F32))
        pTb = e(nc.psum_tensor("pTb", [128, 2 * T], BF16))
        pF1 = e(nc.psum_tensor("pF1", [128, T], F32))
        pF2 = e(nc.psum_tensor("pF2", [128, T], F32))
        pS = e(nc.psum_tensor("pS", [128, T], F32))

        sems = {}
        for s in ["qw", "qx", "pe", "dv", "ac", "w0", "w1", "w2", "so", "tern",
                  "sh2", "shr"]:
            sems[s] = e(nc.semaphore(s))
        QW = Eng(sems["qw"], "qw")
        QX = Eng(sems["qx"], "qx")
        PE = Eng(sems["pe"], "pe", drain=DRAIN_MARKS)
        DV = Eng(sems["dv"], "dv", drain=DRAIN_MARKS)
        SC = Eng(sems["ac"], "ac", drain=DRAIN_MARKS)

        def ld(engq, dst, src):
            return engq.mark(lambda q, d=dst, s=src: q.dma_start(out=d, in_=s), 16)

        ld(QW, idnb[:], idn_bf)
        ld(QW, idnf[:], idn_f32)
        ld(QW, onesb[:], ones_row_bf)
        ld(QW, onesf[:], ones_row_f32)
        ld(QW, zeros, zeros_in)
        ld(QW, cst2, cst2_in)
        for dc in range(DIC):
            ld(QW, A_sb[:, dc * NS:(dc + 1) * NS], Alog[dc * 128:(dc + 1) * 128, :])
            ld(QW, cw1_sb[:, dc * 4:(dc + 1) * 4], cw1[dc * 128:(dc + 1) * 128, :])
        for hc in range(DHC):
            ld(QW, cw2_sb[:, hc * 3:(hc + 1) * 3], cw2[hc * 128:(hc + 1) * 128, :])

        def ld_rowvec(dst, src, nch):
            QX.mark(lambda q, d=dst, s=src, n=nch:
                    q.dma_start(out=d[:, 0:n],
                                in_=s.rearrange("one (c p) -> (one p) c",
                                                p=128),
                                allow_slow_non_contiguous=True), 16)

        ld_rowvec(D_sb, D_in, DIC)
        ld_rowvec(cb1_sb, cb1, DIC)
        ld_rowvec(cb2_sb, cb2, DHC)
        ld_rowvec(fc1b_sb, fc1b, DC)
        ld_rowvec(fc2b_sb, fc2b, DC)
        ld_rowvec(dtb_sb, dtb, DIC)
        for kc in range(DIC):
            QX.mark(lambda q, kc=kc: q.dma_start(
                out=xpT_sb[:, kc * 96:(kc + 1) * 96],
                in_=xpT[kc * 128:(kc + 1) * 128, :]), 16)
        QX.mark(lambda q: q.dma_start(out=dtpT_sb[:, :], in_=dtpT), 16)
        for nm in ("fc1", "inp", "op", "fc2"):
            src_ap, dst_ap = dram["h_" + nm], scr[nm]
            nblk = src_ap.shape[0]
            for b0 in range(nblk):
                QX.mark(lambda q, s=src_ap, d=dst_ap, b0=b0:
                        q.dma_start(out=d[b0], in_=s[b0]), 16)

        su_qw, su_qx = QW.n, QX.n
        QX.self_wait()
        QW.self_wait()
        for E2 in (DV, SC, PE):
            E2.wait(QW, su_qw)
            E2.wait(QX, su_qx)
        scm = SC.mark(lambda q: q.activation(
            A_sb[:, 0:DIC * NS], A_sb[:, 0:DIC * NS], AF.Exp))
        DV.wait(SC, scm)
        DV.mark(lambda q: q.tensor_copy(D_sbf[:], D_sb[:]))
        DV.mark(lambda q: q.tensor_scalar(A_sb[:, 0:DIC * NS], A_sb[:, 0:DIC * NS],
                                          -1.0, None, ALU.mult))

        PHASE[0] = "tern"
        def ternary_build(host_ap, scratch_ap, M, K, cq_col):
            tot = M * K
            ntile = tot // (128 * 4096)
            flat_src = host_ap.rearrange("a b c -> (a b c)") \
                              .rearrange("(r p c) -> r p c", p=128, c=4096)
            flat_dst = scratch_ap.rearrange("a b c -> (a b c)") \
                                 .rearrange("(r p c) -> r p c", p=128, c=4096)
            slot32 = PB.bitcast(F32)
            bfst = PH
            QX.wait(SC, SC.n)
            QX.wait(DV, DV.n)
            DV.mark(lambda q: q.tensor_copy(colA[:, 0:1], zeros[:, 0:1]))
            red_m = {}
            for p in range(ntile):
                sl = slot32[:, (p % 2) * 4096:(p % 2) * 4096 + 4096]
                if p >= 2:
                    QX.wait(DV, red_m[p - 2])
                qxm = QX.mark(lambda q, d=sl, s=flat_src, p=p:
                              q.dma_start(out=d, in_=s[p]), 16)
                QX.self_wait()
                DV.wait(QX, qxm)
                red_m[p] = DV.mark(lambda q, sl=sl: q.tensor_reduce(
                    colB[:, 0:1], sl, AX.X, ALU.add, apply_absolute_value=True))
                DV.mark(lambda q: q.tensor_tensor(
                    colA[:, 0:1], colA[:, 0:1], colB[:, 0:1], ALU.add))
            PE.wait(DV, DV.n)
            pem = PE.mark(lambda q: q.transpose(pS[0:1, 0:128], colA[:, 0:1], idnf[:]))
            DV.wait(PE, pem)
            DV.do(lambda q: q.tensor_reduce(tcol[0:1, 0:1], pS[0:1, 0:128],
                                            AX.X, ALU.add))
            DV.do(lambda q, n=float(tot): q.tensor_scalar(
                tcol[0:1, 0:1], tcol[0:1, 0:1], 1.0 / n, 1e-5, ALU.mult, ALU.max))
            dvm = DV.mark(lambda q: q.reciprocal(tcol[0:1, 1:2], tcol[0:1, 0:1]))
            PE.wait(DV, dvm)
            PE.do(lambda q: q.matmul(pS[:, 0:1], onesf[:], tcol[0:1, 1:2],
                                     start=True, stop=True))
            pem = PE.mark(lambda q: q.matmul(pS[:, 1:2], onesf[:], tcol[0:1, 0:1],
                                             start=True, stop=True))
            DV.wait(PE, pem)
            DV.do(lambda q: q.tensor_copy(swcol[:], pS[:, 0:1]))
            DV.mark(lambda q, c=cq_col: q.tensor_scalar(
                c, pS[:, 1:2], 1.0 / 127.0, None, ALU.mult))
            dv_mean = DV.n
            cast_m = {}
            tern_cnt = getattr(ternary_build, "_cnt", [0])
            ternary_build._cnt = tern_cnt
            for p in range(ntile):
                sl = slot32[:, (p % 2) * 4096:(p % 2) * 4096 + 4096]
                if p < 2:
                    QX.wait(DV, dv_mean)
                else:
                    QX.wait(DV, cast_m[p - 2][0])
                    QX.wait(SC, cast_m[p - 2][1])
                qxm = QX.mark(lambda q, d=sl, s=flat_src, p=p:
                              q.dma_start(out=d, in_=s[p]), 16)
                QX.self_wait()
                DV.wait(QX, qxm)
                DV.do(lambda q, sl=sl: q.tensor_scalar(
                    sl, sl, swcol[:, 0:1], 1.0, ALU.mult, ALU.min))
                DV.do(lambda q, sl=sl: q.tensor_scalar(
                    sl, sl, -1.0, None, ALU.max))
                dvm = DV.mark(lambda q, sl=sl: q.tensor_scalar(
                    sl, sl, 12582912.0, -12582912.0, ALU.add, ALU.add))
                SC.wait(DV, dvm)
                if tern_cnt[0] >= 2:
                    SC.wait_sem(sems["tern"], (tern_cnt[0] - 1) * 16)
                bb = bfst[:, (p % 2) * 4096:(p % 2) * 4096 + 4096]
                scm = SC.mark(lambda q, sl=sl, bb=bb: q.copy(bb, sl))
                cast_m[p] = (dvm, scm)
                QW.wait(SC, scm)
                QW.mark(lambda q, bb=bb, d=flat_dst, p=p:
                        q.dma_start(out=d[p], in_=bb), 16, sem=sems["tern"])
                tern_cnt[0] += 1
                QW.wait_sem(sems["tern"], tern_cnt[0] * 16)
            QX.wait(QW, QW.n)
            return DV.n

        ternary_build(dram["h_w1"], scr["w1"], DH, D, cq1)
        ternary_build(dram["h_w2"], scr["w2"], D, DH, cq2)
        tern_cnt = ternary_build._cnt
        diag_st = PH[:, 0:2 * 512]
        for which, nch, ntap, cw_sb in (("cv1", DIC, 4, cw1_sb),
                                        ("cv2", DHC, 3, cw2_sb)):
            for ch in range(nch):
                slot = ch % 2
                dbuf = diag_st[:, slot * 512: slot * 512 + ntap * 128]
                if tern_cnt[0] >= 2:
                    DV.wait_sem(sems["tern"], (tern_cnt[0] - 1) * 16)
                for tap in range(ntap):
                    fn = (lambda q, dbuf=dbuf, tap=tap, ch=ch, cw_sb=cw_sb,
                          ntap=ntap: q.tensor_scalar(
                              dbuf[:, tap * 128:(tap + 1) * 128], idnb[:],
                              cw_sb[:, ch * ntap + tap:ch * ntap + tap + 1],
                              None, ALU.mult))
                    if tap == ntap - 1:
                        dvm = DV.mark(fn)
                    else:
                        DV.do(fn)
                QW.wait(DV, dvm)
                QW.mark(lambda q, dbuf=dbuf, which=which, ch=ch:
                        q.dma_start(out=scr[which][ch], in_=dbuf), 16,
                        sem=sems["tern"])
                tern_cnt[0] += 1
                QW.wait_sem(sems["tern"], tern_cnt[0] * 16)
        for E2 in (DV, SC, PE, QX):
            E2.wait_sem(sems["tern"], ternary_build._cnt[0] * 16)

        pieces = []
        for mc in range(DC):
            pieces.append(("fc1", mc, D))
        for mc in range(DIC):
            pieces.append(("inp", mc, D))
            pieces.append(("cv1", mc, 4 * 128))
        for mc in range(DIC, 2 * DIC):
            pieces.append(("inp", mc, D))
        for mc in range(DHC):
            pieces.append(("w1", mc, D))
            pieces.append(("cv2", mc, 3 * 128))
        for mc in range(DC):
            pieces.append(("op", mc, DI))
        for mc in range(DC):
            pieces.append(("fc2", mc, D))
        for mc in range(DC):
            pieces.append(("w2", mc, DH))
        NPIECE = len(pieces)
        pr = {"idx": 0}
        piece_ready = []
        piece_free = []

        slot_cnt = [0] * NSLOT

        def issue_piece():
            i = pr["idx"]
            nm, mc, K = pieces[i % NPIECE]
            slot = i % NSLOT
            if i >= NSLOT:
                QW.wait(PE, piece_free[i - NSLOT])
            QW.mark(lambda q, nm=nm, mc=mc, K=K, slot=slot: q.dma_start(
                out=wring[:, slot * 4096: slot * 4096 + K],
                in_=scr[nm][mc]), 16, sem=sems["w%d" % slot])
            slot_cnt[slot] += 1
            piece_ready.append((slot, slot_cnt[slot] * 16))
            piece_free.append(None)
            pr["idx"] += 1

        def stat_ap(i, kc):
            slot = i % NSLOT
            return wring[:, slot * 4096 + kc * 128: slot * 4096 + (kc + 1) * 128]

        for _ in range(NSLOT):
            issue_piece()

        def mm_piece(pi, Kc, mov_fn, pp, extra=None):
            rslot, rcnt = piece_ready[pi]
            PE.wait_sem(sems["w%d" % rslot], rcnt)
            for kc in range(Kc):
                fn = (lambda q, o=pp, s=stat_ap(pi, kc), m=mov_fn(kc),
                      a=(kc == 0), z=(kc == Kc - 1):
                      q.matmul(o[:, 0:T], s, m, start=a, stop=z))
                if kc == Kc - 1 and extra is None:
                    pem = PE.mark(fn)
                else:
                    PE.do(fn)
            if extra is not None:
                po, mov2 = extra
                for kc in range(Kc):
                    fn = (lambda q, o=po, s=stat_ap(pi, kc), m=mov2(kc),
                          a=(kc == 0), z=(kc == Kc - 1):
                          q.matmul(o[:, 0:1], s, m, start=a, stop=z))
                    if kc == Kc - 1:
                        pem = PE.mark(fn)
                    else:
                        PE.do(fn)
            piece_free[pi] = pem
            return pem

        so_cnt = [0]
        sh2_cnt = [0]
        shr_cnt = [0]
        ffn_deq = []
        ffn_conv = []
        ffn_sil = []
        ffn_tree = []
        dbg_seen = set()

        def dbg(name, ap, tile):
            if not DEBUG or tile != 0 or name in dbg_seen:
                return
            dbg_seen.add(name)
            t = nc.dram_tensor("dbg_" + name, list(ap.shape), ap.dtype,
                               kind="ExternalOutput").ap()
            for E2 in (PE, DV, SC):
                QW.wait(E2, E2.n)
            QW.mark(lambda q, t=t, ap=ap: q.dma_start(out=t, in_=ap), 16,
                    sem=sems["so"])
            so_cnt[0] += 1
            QW.wait_sem(sems["so"], so_cnt[0] * 16)
            for E2 in (PE, DV, SC):
                E2.wait_sem(sems["so"], so_cnt[0] * 16)

        for tile in range(NTILES):
            b = tile // NTS
            ts = tile % NTS
            t0 = ts * T
            first = ts == 0
            last = ts == NTS - 1
            has_extra = not last
            NSUBX = NSUB + (1 if has_extra else 0)
            pbase = tile * NPIECE

            def need_piece(pi):
                while pr["idx"] <= pi:
                    issue_piece()

            QX.wait(DV, DV.n)
            DV.wait(PE, PE.n)
            SC.wait(PE, PE.n)
            DV.wait_sem(sems["so"], so_cnt[0] * 16)
            SC.wait_sem(sems["so"], so_cnt[0] * 16)

            PHASE[0] = "tm"
            xm = []
            for su in range(NSUB):
                qm = QX.mark(lambda q, b=b, t0=t0, su=su: q.dma_start(
                    out=x_tm[:, su * D:(su + 1) * D],
                    in_=x_in[b, t0 + su * 128: t0 + (su + 1) * 128, :]), 16)
                xm.append(qm)
            if has_extra:
                qm = QX.mark(lambda q, b=b, t0=t0: q.dma_start(
                    out=x_extra,
                    in_=x_in[b, t0 + T: t0 + T + 1, :]), 16)
                xm.append(qm)
            QX.self_wait()
            xm = [xm[-1]] * len(xm)

            dvm_lnprev = 0
            scm_lnprev = 0
            for su in range(NSUBX):
                r = 128 if su < NSUB else 1
                xs = x_tm[0:r, su * D:(su + 1) * D] if su < NSUB else x_extra
                DV.wait(QX, xm[su])
                DV.wait(SC, scm_lnprev)
                DV.do(lambda q, xs=xs, su=su, r=r: q.tensor_reduce(
                    colA[0:r, su:su + 1], xs, AX.X, ALU.add))
                SC.wait(QX, xm[su])
                SC.wait(DV, dvm_lnprev)
                scm = SC.mark(lambda q, xs=xs, su=su, r=r: q.activation(
                    xsq[0:r, 0:D], xs, AF.Square))
                DV.wait(SC, scm)
                DV.do(lambda q, su=su, r=r: q.tensor_reduce(
                    colB[0:r, su:su + 1], xsq[0:r, 0:D], AX.X, ALU.add))
                DV.do(lambda q, su=su, r=r: q.tensor_scalar(
                    colA[0:r, su:su + 1], colA[0:r, su:su + 1], 1.0 / D,
                    None, ALU.mult))
                DV.do(lambda q, su=su, r=r: q.tensor_scalar(
                    colB[0:r, su:su + 1], colB[0:r, su:su + 1], 1.0 / D,
                    None, ALU.mult))
                DV.do(lambda q, su=su, r=r: q.tensor_tensor(
                    colC[0:r, su:su + 1], colA[0:r, su:su + 1],
                    colA[0:r, su:su + 1], ALU.mult))
                dvm = DV.mark(lambda q, su=su, r=r: q.tensor_tensor(
                    colB[0:r, su:su + 1], colB[0:r, su:su + 1],
                    colC[0:r, su:su + 1], ALU.subtract))
                SC.wait(DV, dvm)
                SC.do(lambda q, su=su, r=r: q.activation(
                    colC[0:r, su:su + 1], colB[0:r, su:su + 1], AF.Ln,
                    bias=epsc[0:r, 0:1], scale=1.0))
                scm = SC.mark(lambda q, su=su, r=r: q.activation(
                    colC[0:r, su:su + 1], colC[0:r, su:su + 1], AF.Exp,
                    scale=-0.5))
                DV.wait(SC, scm)
                DV.do(lambda q, su=su, r=r: q.tensor_tensor(
                    colA[0:r, su:su + 1], colA[0:r, su:su + 1],
                    colC[0:r, su:su + 1], ALU.mult))
                dvm = DV.mark(lambda q, su=su, r=r: q.tensor_scalar(
                    colA[0:r, su:su + 1], colA[0:r, su:su + 1], -1.0,
                    None, ALU.mult))
                SC.wait(DV, dvm)
                scm = SC.mark(lambda q, xs=xs, su=su, r=r: q.activation(
                    xn32[0:r, 0:D], xs, AF.Identity,
                    bias=colA[0:r, su:su + 1], scale=colC[0:r, su:su + 1]))
                DV.wait(SC, scm)
                if su < NSUB:
                    DV.do(lambda q, su=su, r=r: q.tensor_copy(
                        xn_tm[0:r, su * D:(su + 1) * D], xn32[0:r, 0:D]))
                DV.do(lambda q, su=su, r=r: q.tensor_reduce(
                    m1c[0:r, su:su + 1], xn32[0:r, 0:D], AX.X, ALU.max,
                    apply_absolute_value=True))
                DV.do(lambda q, su=su, r=r: q.tensor_scalar(
                    m1c[0:r, su:su + 1], m1c[0:r, su:su + 1], 1e-5, None,
                    ALU.max))
                DV.do(lambda q, su=su, r=r: q.reciprocal(
                    s1c[0:r, su:su + 1], m1c[0:r, su:su + 1]))
                DV.do(lambda q, su=su, r=r: q.tensor_scalar(
                    s1c[0:r, su:su + 1], s1c[0:r, su:su + 1], 127.0, None,
                    ALU.mult))
                DV.do(lambda q, su=su, r=r: q.tensor_tensor(
                    f1c[0:r, su:su + 1], m1c[0:r, su:su + 1], cq1[0:r, 0:1],
                    ALU.mult))
                DV.do(lambda q, su=su, r=r: q.tensor_scalar(
                    qpre_tm[0:r, 0:D], xn32[0:r, 0:D], s1c[0:r, su:su + 1],
                    127.0, ALU.mult, ALU.min))
                DV.do(lambda q, r=r: q.tensor_scalar(
                    qpre_tm[0:r, 0:D], qpre_tm[0:r, 0:D], -128.0, None,
                    ALU.max))
                dvm = DV.mark(lambda q, r=r: q.tensor_scalar(
                    qpre_tm[0:r, 0:D], qpre_tm[0:r, 0:D], 12582912.0,
                    -12582912.0, ALU.add, ALU.add))
                SC.wait(DV, dvm)
                scm_lnprev = SC.mark(lambda q, su=su, r=r: q.copy(
                    xq_tm[0:r, su * D:(su + 1) * D], qpre_tm[0:r, 0:D]))
                dvm_lnprev = dvm
            sc_tm = SC.n
            dv_tm = DV.n
            dbg("x_tm", x_tm, tile)
            dbg("xn_tm", xn_tm, tile)
            dbg("xq_tm", xq_tm[:, 0:NSUB * D], tile)
            dbg("m1c", m1c[:, 0:NSUB], tile)

            PE.wait(SC, sc_tm)
            PE.wait(DV, dv_tm)
            for src_tm, dst_fm, stride, w_extra in (
                    (xn_tm, xn_fm, T, False), (xq_tm, xq_fm, TX, True)):
                for dc_i in range(DC):
                    xtr = w_extra and has_extra
                    for su in range(NSUB):
                        fn = (lambda q, s=src_tm, dc_i=dc_i, su=su: q.transpose(
                            pTb[:, su * 128:(su + 1) * 128],
                            s[:, su * D + dc_i * 128: su * D + (dc_i + 1) * 128],
                            idnb[:]))
                        if su == NSUB - 1 and not xtr:
                            pem = PE.mark(fn)
                        else:
                            PE.do(fn)
                    ncols = T
                    if xtr:
                        pem = PE.mark(lambda q, s=src_tm, dc_i=dc_i: q.transpose(
                            pTb[:, T:T + 1],
                            s[0:1, NSUB * D + dc_i * 128:
                              NSUB * D + (dc_i + 1) * 128], idnb[0:1, 0:1]))
                        ncols = T + 1
                    DV.wait(PE, pem)
                    dvm = DV.mark(lambda q, d=dst_fm, dc_i=dc_i, st=stride,
                                  nn=ncols: q.tensor_copy(
                                      d[:, dc_i * st:dc_i * st + nn],
                                      pTb[:, 0:nn]))
                    PE.wait(DV, dvm)
            if not has_extra:
                DV.mark(lambda q: q.tensor_copy(
                    xq_fm[:].rearrange("p (c t) -> p c t", t=TX)[:, :, T:T + 1],
                    zbf[:, 0:DC].unsqueeze(2)))

            dbg("xn_fm", xn_fm, tile)
            dbg("xq_fm", xq_fm, tile)
            PHASE[0] = "fm_mm"
            ev = {}
            ROT = (pA, pB, pF1, pF2)
            for mc in range(DC):
                pi = pbase + mc
                need_piece(pi)
                pp = ROT[mc % 4]
                if mc >= 4:
                    PE.wait(SC, ev[mc - 4])
                pem = mm_piece(pi, DC, lambda kc: xn_fm[:, kc * T:(kc + 1) * T],
                               pp)
                SC.wait(PE, pem)
                ev[mc] = SC.mark(lambda q, pp=pp, mc=mc: q.activation(
                    hfc1[:, mc * T:(mc + 1) * T], pp[:, 0:T], AF.Identity,
                    bias=fc1b_sb[:, mc:mc + 1], scale=1.0))
            PE.wait(SC, ev[DC - 1])

            silu_marks = {}
            ev = {}
            for mc in range(2 * DIC):
                if mc < DIC:
                    pi = pbase + DC + 2 * mc
                else:
                    pi = pbase + DC + DIC + mc
                need_piece(pi)
                pp = pA if mc % 2 == 0 else pB
                if mc >= 2:
                    PE.wait(SC, ev[mc - 2])
                pem = mm_piece(pi, DC, lambda kc: hfc1[:, kc * T:(kc + 1) * T],
                               pp)
                if mc >= DIC:
                    zc = mc - DIC
                    SC.wait(PE, pem)
                    ev[mc] = SC.mark(lambda q, pp=pp, zc=zc: q.activation(
                        sz_sl[:, zc * T:(zc + 1) * T], pp[:, 0:T], AF.Silu))
                else:
                    dc_i = mc
                    slot = dc_i % 2
                    xcb = xc_ring[:, slot * (T + 3): slot * (T + 3) + T + 3]
                    SC.wait(PE, pem)
                    scm = SC.mark(lambda q, pp=pp, xcb=xcb: q.activation(
                        xcb[:, 3:3 + T], pp[:, 0:T], AF.Identity))
                    ev[mc] = scm
                    DV.wait(SC, scm)
                    if first:
                        DV.do(lambda q, xcb=xcb: q.tensor_copy(
                            xcb[:, 0:3], zbf[:, 0:3]))
                    else:
                        DV.do(lambda q, xcb=xcb, dc_i=dc_i: q.tensor_copy(
                            xcb[:, 0:3], xc_tail[:, dc_i * 3:dc_i * 3 + 3]))
                    dvm = DV.mark(lambda q, xcb=xcb, dc_i=dc_i: q.tensor_copy(
                        xc_tail[:, dc_i * 3:dc_i * 3 + 3], xcb[:, T:T + 3]))
                    cpi = pbase + DC + 2 * mc + 1
                    need_piece(cpi)
                    pcv = pC if dc_i % 2 == 0 else pS
                    if dc_i >= 2:
                        PE.wait(SC, silu_marks[dc_i - 2])
                    PE.wait(DV, dvm)
                    rs, rc = piece_ready[cpi]
                    PE.wait_sem(sems["w%d" % rs], rc)
                    for tap in range(4):
                        fn = (lambda q, pcv=pcv, cpi=cpi, tap=tap, xcb=xcb:
                              q.matmul(pcv[:, 0:T], stat_ap(cpi, tap),
                                       xcb[:, tap:tap + T], start=(tap == 0),
                                       stop=(tap == 3)))
                        if tap == 3:
                            cpem = PE.mark(fn)
                        else:
                            PE.do(fn)
                    piece_free[cpi] = cpem
                    SC.wait(PE, cpem)
                    silu_marks[dc_i] = SC.mark(
                        lambda q, pcv=pcv, dc_i=dc_i: q.activation(
                            u_sl[:, dc_i * T:(dc_i + 1) * T], pcv[:, 0:T],
                            AF.Silu, bias=cb1_sb[:, dc_i:dc_i + 1], scale=1.0))
            PE.wait(SC, ev[2 * DIC - 1])
            PE.wait(SC, silu_marks[DIC - 1])
            PE.wait(SC, silu_marks[DIC - 2])
            sc_u = SC.n
            dbg("hfc1", hfc1, tile)
            dbg("u_sl", u_sl, tile)
            dbg("sz_sl", sz_sl, tile)

            PE.wait(SC, sc_u)
            for kc in range(DIC):
                fn = (lambda q, kc=kc, a=(kc == 0), z=(kc == DIC - 1):
                      q.matmul(pX[0:96, 0:T], xpT_sb[:, kc * 96:(kc + 1) * 96],
                               u_sl[:, kc * T:(kc + 1) * T], start=a, stop=z))
                if kc == DIC - 1:
                    pem = PE.mark(fn)
                else:
                    PE.do(fn)
            SC.wait(PE, pem)
            scm = SC.mark(lambda q: q.activation(xdbl[:, 0:T], pX[0:96, 0:T],
                                                 AF.Identity))
            DV.wait(SC, scm)
            dvm = DV.mark(lambda q: q.tensor_copy(dtlow[:, 0:T], xdbl[0:64, 0:T]))
            QX.wait(SC, scm)
            for row in range(32):
                qxm = QX.mark(lambda q, row=row: q.dma_start(
                    out=bc0[0:1, row * T:(row + 1) * T],
                    in_=xdbl[64 + row:65 + row, 0:T]), 16)
            QX.self_wait()
            PE.wait(DV, dvm)
            PE.wait(QX, qxm)

            for half, dstsl in ((0, Bb_sl), (1, Cb_sl)):
                for n in range(NS):
                    row = half * NS + n
                    pem = PE.mark(lambda q, row=row: q.matmul(
                        pC[:, 0:T], onesb[:],
                        bc0[0:1, row * T:(row + 1) * T],
                        start=True, stop=True))
                    DV.wait(PE, pem)
                    dvm = DV.mark(lambda q, d=dstsl, n=n: q.tensor_copy(
                        d[:, n * T:(n + 1) * T], pC[:, 0:T]))
                    PE.wait(DV, dvm)

            dbg("xdbl", xdbl, tile)
            dbg("Bb_sl", Bb_sl, tile)
            dbg("Cb_sl", Cb_sl, tile)

            PE.wait(DV, DV.n)
            for su in range(NSUBX):
                r = 128 if su < NSUB else 1
                pem = PE.mark(lambda q, su=su, r=r: q.transpose(
                    pS[0:1, 0:r], f1c[0:r, su:su + 1], idnf[0:r, 0:r]))
                DV.wait(PE, pem)
                dvm = DV.mark(lambda q, su=su, r=r: q.tensor_copy(
                    srow[0:1, su * 128:su * 128 + r], pS[0:1, 0:r]))
                PE.wait(DV, dvm)
            Tf = T + (1 if has_extra else 0)
            dvm = DV.mark(lambda q, Tf=Tf: q.tensor_copy(
                srow_bf[0:1, 0:Tf], srow[0:1, 0:Tf]))
            PE.wait(DV, dvm)
            pem = PE.mark(lambda q: q.matmul(pC[:, 0:T], onesb[:],
                                             srow_bf[0:1, 0:T],
                                             start=True, stop=True))
            DV.wait(PE, pem)
            dvm = DV.mark(lambda q: q.tensor_copy(f1b[:, 0:T], pC[:, 0:T]))
            PE.wait(DV, dvm)
            if has_extra:
                pem = PE.mark(lambda q: q.matmul(pC[:, 0:1], onesb[:],
                                                 srow_bf[0:1, T:T + 1],
                                                 start=True, stop=True))
                DV.wait(PE, pem)
                dvm = DV.mark(lambda q: q.tensor_copy(f1b[:, T:T + 1],
                                                      pC[:, 0:1]))
                PE.wait(DV, dvm)
            else:
                DV.mark(lambda q: q.tensor_copy(f1b[:, T:T + 1],
                                                zbf[:, 0:1]))
            PE.wait(DV, DV.n)

            def emit_ffn_unit(mc):
                g = len(ffn_conv)
                pi = pbase + DC + 3 * DIC + 2 * mc
                need_piece(pi)
                pp = pF1 if mc % 2 == 0 else pF2
                px = pS if mc % 2 == 0 else pC
                if g >= 2:
                    PE.wait(DV, ffn_deq[g - 2])
                pem = mm_piece(
                    pi, DC, lambda kc: xq_fm[:, kc * TX:kc * TX + T], pp,
                    extra=(px, lambda kc: xq_fm[:, kc * TX + T:kc * TX + T + 1]))
                sl = g % 4
                h1b = h1r[:, sl * (T + 2):sl * (T + 2) + T + 2]
                DV.wait(PE, pem)
                if g >= 4:
                    DV.wait(PE, ffn_conv[g - 4])
                DV.do(lambda q, pp=pp, h1b=h1b: q.tensor_tensor(
                    h1b[:, 1:1 + T], pp[:, 0:T], f1b[:, 0:T], ALU.mult))
                evd = DV.mark(lambda q, h1b=h1b, px=px: q.tensor_tensor(
                    h1b[:, T + 1:T + 2], px[:, 0:1], f1b[:, T:T + 1],
                    ALU.mult))
                ffn_deq.append(evd)
                if first:
                    DV.do(lambda q, h1b=h1b: q.tensor_copy(
                        h1b[:, 0:1], zbf[:, 0:1]))
                else:
                    DV.do(lambda q, h1b=h1b, mc=mc: q.tensor_copy(
                        h1b[:, 0:1], h1_tail[:, mc:mc + 1]))
                dvm = DV.mark(lambda q, h1b=h1b, mc=mc: q.tensor_copy(
                    h1_tail[:, mc:mc + 1], h1b[:, T:T + 1]))
                cpi = pbase + DC + 3 * DIC + 2 * mc + 1
                need_piece(cpi)
                pcv = pA if mc % 2 == 0 else pB
                if g >= 2:
                    PE.wait(SC, ffn_sil[g - 2])
                PE.wait(DV, dvm)
                rs, rc = piece_ready[cpi]
                PE.wait_sem(sems["w%d" % rs], rc)
                for tap in range(3):
                    fn = (lambda q, pcv=pcv, cpi=cpi, tap=tap, h1b=h1b:
                          q.matmul(pcv[:, 0:T], stat_ap(cpi, tap),
                                   h1b[:, tap:tap + T], start=(tap == 0),
                                   stop=(tap == 2)))
                    if tap == 2:
                        cpem = PE.mark(fn)
                    else:
                        PE.do(fn)
                piece_free[cpi] = cpem
                ffn_conv.append(cpem)
                h2b = h2r[:, (g % 3) * T:(g % 3 + 1) * T]
                SC.wait(PE, cpem)
                if g >= 3:
                    SC.wait_sem(sems["sh2"], (g - 2) * 16)
                    SC.wait(DV, ffn_tree[g - 3])
                silm = SC.mark(lambda q, pcv=pcv, h2b=h2b, mc=mc: q.activation(
                    h2b[:], pcv[:, 0:T], AF.Silu,
                    bias=cb2_sb[:, mc:mc + 1], scale=1.0))
                ffn_sil.append(silm)
                DV.wait(SC, silm)
                if mc == 0:
                    DV.do(lambda q, h2b=h2b: q.tensor_copy(mx_t[:], h2b[:]))
                    trm = DV.mark(lambda q, h2b=h2b: q.tensor_copy(
                        mn_t[:], h2b[:]))
                else:
                    DV.do(lambda q, h2b=h2b: q.tensor_tensor(
                        mx_t[:], mx_t[:], h2b[:], ALU.max))
                    trm = DV.mark(lambda q, h2b=h2b: q.tensor_tensor(
                        mn_t[:], mn_t[:], h2b[:], ALU.min))
                ffn_tree.append(trm)
                QX.wait(SC, silm)
                if mc == 0:
                    QX.wait_sem(sems["shr"], shr_cnt[0] * 16)
                QX.mark(lambda q, mc=mc, h2b=h2b: q.dma_start(
                    out=s_h2[mc], in_=h2b[:]), 16, sem=sems["sh2"])
                sh2_cnt[0] += 1
                QX.wait_sem(sems["sh2"], sh2_cnt[0] * 16)

            PHASE[0] = "scan"
            if first:
                DV.do(lambda q: q.tensor_copy(st_sl[:], zeros[:]))
            dtu_marks = {}
            exp_marks = {}
            scan_hist = []
            b_grp = PD[:, 0:4 * T]
            for dc_i in range(DIC):
                par = dc_i % 2
                pem = PE.mark(lambda q, dc_i=dc_i: q.matmul(
                    pX[:, 0:T], dtpT_sb[:, dc_i * 128:(dc_i + 1) * 128],
                    dtlow[:, 0:T], start=True, stop=True))
                DV.wait(PE, pem)
                if dc_i >= 2:
                    DV.wait(SC, exp_marks[dc_i - 2])
                dtr = dtr_r[:, par * T:par * T + T]
                dvm = DV.mark(lambda q, dtr=dtr, dc_i=dc_i: q.tensor_scalar(
                    dtr, pX[:, 0:T], dtb_sb[:, dc_i:dc_i + 1], 80.0,
                    ALU.add, ALU.min))
                PE.wait(DV, dvm)
                SC.wait(DV, dvm)
                if dc_i >= 2:
                    SC.wait(DV, dtu_marks[dc_i - 2])
                esp = esp_r[:, par * T:par * T + T]
                dtc = dt_r[:, par * T:par * T + T]
                SC.do(lambda q, dtr=dtr, esp=esp: q.activation(esp, dtr, AF.Exp))
                exp_marks[dc_i] = SC.mark(lambda q, esp=esp, dtc=dtc: q.activation(
                    dtc, esp, AF.Ln, bias=onec[:, 0:1], scale=1.0))
                DV.wait(SC, exp_marks[dc_i])
                dtuc = dtu_r[:, par * T:par * T + T]
                dtu_marks[dc_i] = DV.mark(
                    lambda q, dtc=dtc, dtuc=dtuc, dc_i=dc_i: q.tensor_tensor(
                        dtuc, dtc, u_sl[:, dc_i * T:(dc_i + 1) * T], ALU.mult))
                for n in range(NS):
                    if n % 4 == 0:
                        g = n // 4
                        DV.mark(lambda q, dtuc=dtuc, g=g: q.tensor_tensor(
                            b_grp.rearrange("p (n t) -> p n t", t=T),
                            dtuc.unsqueeze(1).broadcast_to([128, 4, T]),
                            Bb_sl[:, g * 4 * T:(g + 1) * 4 * T]
                            .rearrange("p (n t) -> p n t", t=T), ALU.mult))
                    asl = n % 4
                    astr = a_ring[:, asl * T:asl * T + T]
                    if len(scan_hist) >= 4:
                        SC.wait(DV, scan_hist[-4])
                    scm = SC.mark(lambda q, astr=astr, dtc=dtc, dc_i=dc_i, n=n:
                                  q.activation(astr, dtc, AF.Exp,
                                               scale=A_sb[:, dc_i * NS + n:
                                                          dc_i * NS + n + 1]))
                    DV.wait(SC, scm)
                    init_ap = st_sl[:, dc_i * NS + n: dc_i * NS + n + 1]
                    scan_hist.append(DV.mark(
                        lambda q, astr=astr, asl=asl, n=n, i0=init_ap:
                        q.tensor_tensor_scan(h_sl[:, n * T:(n + 1) * T],
                                             astr,
                                             b_grp[:, asl * T:(asl + 1) * T],
                                             i0, ALU.mult, ALU.add)))
                DV.do(lambda q, dc_i=dc_i: q.tensor_copy(
                    st_sl[:, dc_i * NS:(dc_i + 1) * NS],
                    h_sl[:].rearrange("p (n t) -> p n t", t=T)[:, :, T - 1:T]
                    .rearrange("p n one -> p (n one)")))
                DV.do(lambda q: q.tensor_tensor(h_sl[:], h_sl[:], Cb_sl[:],
                                                ALU.mult))
                w = NS * T // 2
                while w > T:
                    DV.do(lambda q, w=w: q.tensor_tensor(
                        h_sl[:, 0:w], h_sl[:, 0:w], h_sl[:, w:2 * w], ALU.add))
                    w //= 2
                DV.mark(lambda q, dc_i=dc_i: q.tensor_tensor(
                    y_sl[:, dc_i * T:(dc_i + 1) * T], h_sl[:, 0:T],
                    h_sl[:, T:2 * T], ALU.add))
                emit_ffn_unit(2 * dc_i)
                emit_ffn_unit(2 * dc_i + 1)
            DV.do(lambda q: q.tensor_tensor(
                h_sl[:].rearrange("p (n t) -> p n t", t=T),
                u_sl.rearrange("p (c t) -> p c t", t=T),
                D_sbf.unsqueeze(2).broadcast_to([128, DIC, T]), ALU.mult))
            DV.do(lambda q: q.tensor_tensor(y_sl, y_sl, h_sl[:, 0:DIC * T],
                                            ALU.add))
            DV.mark(lambda q: q.tensor_tensor(y_sl, y_sl, sz_sl, ALU.mult))
            dv_y = DV.n
            dbg("y_sl", y_sl, tile)
            dbg("st_sl", st_sl, tile)

            PHASE[0] = "post"
            QX.wait(DV, dv_y)
            QX.wait_sem(sems["sh2"], sh2_cnt[0] * 16)
            QX.mark(lambda q: q.dma_start(
                out=h2_sl[:].rearrange("p (m t) -> p m t", t=T),
                in_=s_h2.rearrange("m p t -> p m t")),
                16, sem=sems["shr"])
            shr_cnt[0] += 1

            PE.wait(DV, dv_y)
            ev = {}
            for mc in range(DC):
                pi = pbase + DC + 3 * DIC + 2 * DHC + mc
                need_piece(pi)
                pp = ROT[mc % 4]
                if mc >= 4:
                    PE.wait(SC, ev[mc - 4])
                pem = mm_piece(pi, DIC, lambda kc: y_sl[:, kc * T:(kc + 1) * T],
                               pp)
                SC.wait(PE, pem)
                ev[mc] = SC.mark(lambda q, pp=pp, mc=mc: q.activation(
                    attnm[:, mc * T:(mc + 1) * T], pp[:, 0:T], AF.Identity))
            PE.wait(SC, ev[DC - 1])

            ev = {}
            for mc in range(DC):
                pi = pbase + DC + 3 * DIC + 2 * DHC + DC + mc
                need_piece(pi)
                pp = ROT[mc % 4]
                if mc >= 4:
                    PE.wait(SC, ev[mc - 4])
                pem = mm_piece(pi, DC, lambda kc: attnm[:, kc * T:(kc + 1) * T],
                               pp)
                SC.wait(PE, pem)
                ev[mc] = SC.mark(lambda q, pp=pp, mc=mc: q.activation(
                    attn32[:, mc * T:(mc + 1) * T], pp[:, 0:T], AF.Identity,
                    bias=fc2b_sb[:, mc:mc + 1], scale=1.0))
            PE.wait(SC, ev[DC - 1])

            dbg("attn32", attn32, tile)
            PHASE[0] = "ffn"
            dbg("f1b", f1b, tile)
            DV.do(lambda q: q.tensor_scalar(mn_t[:, 0:T], mn_t[:, 0:T], -1.0,
                                            None, ALU.mult))
            dvm = DV.mark(lambda q: q.tensor_tensor(mx_t[:, 0:T], mx_t[:, 0:T],
                                                    mn_t[:, 0:T], ALU.max))
            PE.wait(DV, dvm)
            for su in range(NSUB):
                pem = PE.mark(lambda q, su=su: q.transpose(
                    pTb[:, 0:128], mx_t[:, su * 128:(su + 1) * 128], idnb[:]))
                DV.wait(PE, pem)
                dvm = DV.mark(lambda q, su=su: q.tensor_reduce(
                    m2c[:, su:su + 1], pTb[:, 0:128], AX.X, ALU.max))
                PE.wait(DV, dvm)
            DV.do(lambda q: q.tensor_scalar(m2c[:, 0:NSUB], m2c[:, 0:NSUB],
                                            1e-5, None, ALU.max))
            DV.do(lambda q: q.reciprocal(s2c[:, 0:NSUB], m2c[:, 0:NSUB]))
            DV.do(lambda q: q.tensor_scalar(s2c[:, 0:NSUB], s2c[:, 0:NSUB],
                                            127.0, None, ALU.mult))
            dvm = DV.mark(lambda q: q.tensor_tensor(
                f2c[:, 0:NSUB], m2c[:, 0:NSUB],
                cq2[:, 0:1].broadcast_to([128, NSUB]), ALU.mult))
            PE.wait(DV, dvm)
            for colsrc, dstb in ((s2c, s2b), (f2c, f2b)):
                for su in range(NSUB):
                    pem = PE.mark(lambda q, c=colsrc, su=su: q.transpose(
                        pS[0:1, 0:128], c[:, su:su + 1], idnf[:]))
                    DV.wait(PE, pem)
                    dvm = DV.mark(lambda q, su=su: q.tensor_copy(
                        srow[0:1, su * 128:(su + 1) * 128], pS[0:1, 0:128]))
                    PE.wait(DV, dvm)
                dvm = DV.mark(lambda q: q.tensor_copy(srow_bf[0:1, 0:T],
                                                      srow[0:1, 0:T]))
                PE.wait(DV, dvm)
                pem = PE.mark(lambda q: q.matmul(pC[:, 0:T], onesb[:],
                                                 srow_bf[0:1, 0:T],
                                                 start=True, stop=True))
                DV.wait(PE, pem)
                dvm = DV.mark(lambda q, d=dstb: q.tensor_copy(d[:, 0:T],
                                                              pC[:, 0:T]))
                PE.wait(DV, dvm)

            DV.wait_sem(sems["shr"], shr_cnt[0] * 16)
            cast_marks = {}
            qp_full = PD.bitcast(F32)
            NB = 4 * T
            for blk in range(DHC * T // NB):
                qp = qp_full[:, 0:NB] if False else qp_full
                if blk >= 1:
                    DV.wait(SC, cast_marks[blk - 1])
                DV.do(lambda q, blk=blk: q.tensor_tensor(
                    qp_full.rearrange("p (c t) -> p c t", t=T),
                    h2_sl[:, blk * NB:(blk + 1) * NB]
                    .rearrange("p (c t) -> p c t", t=T),
                    s2b.unsqueeze(1).broadcast_to([128, 4, T]), ALU.mult))
                DV.do(lambda q: q.tensor_scalar(
                    qp_full, qp_full, 127.0, -128.0, ALU.min, ALU.max))
                dvm = DV.mark(lambda q: q.tensor_scalar(
                    qp_full, qp_full, 12582912.0, -12582912.0, ALU.add,
                    ALU.add))
                SC.wait(DV, dvm)
                cast_marks[blk] = SC.mark(lambda q, blk=blk: q.copy(
                    h2_sl[:, blk * NB:(blk + 1) * NB], qp_full))
            sc_q2 = SC.n

            dbg("q2b", h2_sl, tile)
            dbg("m2c", m2c, tile)
            dbg("s2b", s2b, tile)
            dbg("f2b", f2b, tile)
            PE.wait(SC, sc_q2)
            ev = {}
            for mc in range(DC):
                pi = pbase + DC + 3 * DIC + 2 * DC + 2 * DHC + mc
                need_piece(pi)
                pp = ROT[mc % 4]
                if mc >= 4:
                    PE.wait(DV, ev[mc - 4])
                pem = mm_piece(pi, DHC,
                               lambda kc: h2_sl[:, kc * T:(kc + 1) * T], pp)
                DV.wait(PE, pem)
                ev[mc] = DV.mark(lambda q, pp=pp, mc=mc: q.tensor_tensor(
                    ffn32[:, mc * T:(mc + 1) * T], pp[:, 0:T], f2b[:, 0:T],
                    ALU.mult))
            PE.wait(DV, ev[DC - 1])

            dbg("ffn32", ffn32, tile)
            PHASE[0] = "out"
            for su in range(NSUB):
                for half in range(2):
                    for j in range(4):
                        dci = half * 4 + j
                        fn = (lambda q, su=su, dci=dci, j=j: q.transpose(
                            pS[:, j * 128:(j + 1) * 128],
                            attn32[:, dci * T + su * 128:
                                   dci * T + (su + 1) * 128], idnf[:]))
                        if j == 3:
                            pem = PE.mark(fn)
                        else:
                            PE.do(fn)
                    DV.wait(PE, pem)
                    off = su * D + half * 512
                    dvm = DV.mark(lambda q, off=off: q.scalar_tensor_tensor(
                        out_tm[:, off:off + 512], pS[:, 0:512], onec[:, 0:1],
                        x_tm[:, off:off + 512], ALU.mult, ALU.add))
                    PE.wait(DV, dvm)
                    for j in range(4):
                        dci = half * 4 + j
                        fn = (lambda q, su=su, dci=dci, j=j: q.transpose(
                            pS[:, j * 128:(j + 1) * 128],
                            ffn32[:, dci * T + su * 128:
                                  dci * T + (su + 1) * 128], idnf[:]))
                        if j == 3:
                            pem = PE.mark(fn)
                        else:
                            PE.do(fn)
                    DV.wait(PE, pem)
                    dvm = DV.mark(lambda q, off=off: q.tensor_tensor(
                        out_tm[:, off:off + 512], out_tm[:, off:off + 512],
                        pS[:, 0:512], ALU.add))
                    PE.wait(DV, dvm)
                QW.wait(DV, DV.n)
                QW.mark(lambda q, b=b, t0=t0, su=su: q.dma_start(
                    out=out_d[b, t0 + su * 128: t0 + (su + 1) * 128, :],
                    in_=out_tm[:, su * D:(su + 1) * D]), 16, sem=sems["so"])
                so_cnt[0] += 1
            QW.wait_sem(sems["so"], so_cnt[0] * 16)

        QW.do(lambda q, c=None: q.wait_ge(sems["so"], so_cnt[0] * 16))

        block = e(nc.Block())

        @block.sync
        def _(q):
            for fn in QW.ops:
                fn(q)

        @block.gpsimd
        def _(q):
            for fn in QX.ops:
                fn(q)

        @block.tensor
        def _(q):
            for fn in PE.ops:
                fn(q)

        @block.vector
        def _(q):
            for fn in DV.ops:
                fn(q)

        @block.scalar
        def _(q):
            for fn in SC.ops:
                fn(q)

    return nc


def _blocked(w):
    M, K = w.shape
    Mc, Kc = M // 128, K // 128
    v = w.reshape(Mc, 128, Kc, 128).transpose(0, 3, 2, 1)
    return np.ascontiguousarray(v.reshape(Mc, 128, K))


def kernel(L0_attn_x, L0_attn_fc1_w, L0_attn_fc1_b, L0_attn_in_proj_w,
           L0_attn_conv_w, L0_attn_conv_b, L0_attn_x_proj_w, L0_attn_dt_proj_w,
           L0_attn_dt_proj_b, L0_attn_A_log, L0_attn_D, L0_attn_out_proj_w,
           L0_attn_fc2_w, L0_attn_fc2_b, L0_ffn_w1, L0_ffn_conv_w,
           L0_ffn_conv_b, L0_ffn_w2):
    f = np.float32
    c = np.ascontiguousarray
    x = np.asarray(L0_attn_x, f)

    common = {
        "h_fc1": _blocked(np.asarray(L0_attn_fc1_w, f)),
        "h_inp": _blocked(np.asarray(L0_attn_in_proj_w, f)),
        "h_op": _blocked(np.asarray(L0_attn_out_proj_w, f)),
        "h_fc2": _blocked(np.asarray(L0_attn_fc2_w, f)),
        "h_w1": _blocked(np.asarray(L0_ffn_w1, f)),
        "h_w2": _blocked(np.asarray(L0_ffn_w2, f)),
        "fc1b": c(np.asarray(L0_attn_fc1_b, f)[None, :]),
        "fc2b": c(np.asarray(L0_attn_fc2_b, f)[None, :]),
        "cw1": c(np.asarray(L0_attn_conv_w, f)),
        "cb1": c(np.asarray(L0_attn_conv_b, f)[None, :]),
        "xpT": c(np.asarray(L0_attn_x_proj_w, f).T),
        "dtpT": c(np.asarray(L0_attn_dt_proj_w, f).T),
        "dtb": c(np.asarray(L0_attn_dt_proj_b, f)[None, :]),
        "Alog": c(np.asarray(L0_attn_A_log, f)),
        "Dvec": c(np.asarray(L0_attn_D, f)[None, :]),
        "cw2": c(np.asarray(L0_ffn_conv_w, f)[:, 0, :]),
        "cb2": c(np.asarray(L0_ffn_conv_b, f)[None, :]),
        "idn_bf": np.eye(128, dtype=NBF),
        "idn_f32": np.eye(128, dtype=f),
        "ones_row_bf": np.ones((1, 128), NBF),
        "ones_row_f32": np.ones((1, 128), f),
        "zeros_in": np.zeros((128, 256), f),
        "cst2_in": c(np.stack([np.full(128, 1e-5, f),
                               np.ones(128, f)], axis=1)),
    }

    nc = build_program()
    in_maps = []
    for core in range(NCORES):
        m = dict(common)
        m["x"] = c(x[core * BC:(core + 1) * BC])
        in_maps.append(m)
    res = run_bass_kernel_spmd(nc, in_maps, core_ids=list(range(NCORES)))
    return np.concatenate([res.results[cc]["out"] for cc in range(NCORES)],
                          axis=0)


# version 11 (final)
# speedup vs baseline: 1.0623x; 1.0623x over previous
from contextlib import ExitStack

import numpy as np
import ml_dtypes

import concourse.bass as bass
import concourse.mybir as mybir
from concourse.bass_utils import run_bass_kernel_spmd

F32 = mybir.dt.float32
BF16 = mybir.dt.bfloat16
I32 = mybir.dt.int32
AF = mybir.ActivationFunctionType
ALU = mybir.AluOpType
AX = mybir.AxisListType
NBF = ml_dtypes.bfloat16

NCORES = 8
BC = 4
L = 2048
D = 1024
DI = 2048
DH = 4096
NS = 16
T = 512
NTS = L // T
NTILES = BC * NTS
NSUB = T // 128
DC = D // 128
DIC = DI // 128
DHC = DH // 128
TX = T + 1

DEBUG = False
AUDIT = True
AUDIT_ENGINES = {"dv"}
AUDIT_CUTOFF = None
AUDIT_INC_ONLY = False
AUDIT_DEPTH = 1
AUDIT_PHASES = None
DRAIN_MARKS = False
RACE_DETECT = False
PHASE = ["startup"]

BIGW = [("fc1", D, D), ("inp", 2 * DI, D), ("op", D, DI), ("fc2", D, D),
        ("w1", DH, D), ("w2", D, DH)]


class Eng:
    def __init__(self, sem, name="", drain=False):
        self.sem = sem
        self.name = name
        self.drain = drain
        self.ops = []
        self.n = 0

    def _audit(self):
        if not (AUDIT and (AUDIT_ENGINES is None or self.name in AUDIT_ENGINES)):
            return False
        if AUDIT_PHASES is not None and PHASE[0] not in AUDIT_PHASES:
            return False
        return AUDIT_CUTOFF is None or len(self.ops) < AUDIT_CUTOFF

    def do(self, fn):
        if self._audit():
            self.mark(fn)
        else:
            self.ops.append(fn)

    def mark(self, fn, k=1, sem=None):
        sem = self.sem if sem is None else sem
        if self._audit() and sem is self.sem and not AUDIT_INC_ONLY:
            prior = self.n - (AUDIT_DEPTH - 1)
            self.ops.append(lambda q, s=self.sem, c=prior: q.wait_ge(s, c)
                            if c > 0 else None)
        if self._audit() and sem is not self.sem:
            self.ops.append(
                lambda q, fn=fn, sem=sem, k=k, own=self.sem, c=self.n:
                (q.wait_ge(own, c) if c > 0 else None,
                 fn(q).then_inc(sem, k))[-1])
            return None
        if self.drain:
            self.ops.append(lambda q, fn=fn: fn(q))
            self.ops.append(
                lambda q, sem=sem, k=k: q.drain().then_inc(sem, k))
        else:
            self.ops.append(
                lambda q, fn=fn, sem=sem, k=k: fn(q).then_inc(sem, k))
        if sem is self.sem:
            self.n += k
            return self.n
        return None

    def wait_sem(self, sem, cnt):
        if cnt > 0:
            self.ops.append(lambda q, s=sem, c=cnt: q.wait_ge(s, c))

    def wait(self, other, cnt):
        if cnt > 0:
            self.ops.append(lambda q, s=other.sem, c=cnt: q.wait_ge(s, c))

    def self_wait(self):
        self.ops.append(lambda q, s=self.sem, c=self.n: q.wait_ge(s, c))


def build_program():
    nc = bass.Bass("TRN2", target_bir_lowering=False,
                   detect_race_conditions=RACE_DETECT)
    dram = {}

    def din(name, shape, dt=F32):
        dram[name] = nc.dram_tensor(name, shape, dt, kind="ExternalInput").ap()
        return dram[name]

    x_in = din("x", [BC, L, D])
    for nm, M, K in BIGW:
        din("h_" + nm, [M // 128, 128, K])
    cw1 = din("cw1", [DI, 4])
    cb1 = din("cb1", [1, DI])
    xpT = din("xpT", [DI, 96])
    dtpT = din("dtpT", [64, DI])
    dtb = din("dtb", [1, DI])
    Alog = din("Alog", [DI, NS])
    D_in = din("Dvec", [1, DI])
    cw2 = din("cw2", [DH, 3])
    cb2 = din("cb2", [1, DH])
    fc1b = din("fc1b", [1, D])
    fc2b = din("fc2b", [1, D])
    idn_bf = din("idn_bf", [128, 128], BF16)
    idn_f32 = din("idn_f32", [128, 128])
    ones_row_bf = din("ones_row_bf", [1, 128], BF16)
    ones_row_f32 = din("ones_row_f32", [1, 128])
    zeros_in = din("zeros_in", [128, 256])
    cst2_in = din("cst2_in", [128, 2])

    out_d = nc.dram_tensor("out", [BC, L, D], F32, kind="ExternalOutput").ap()
    scr = {}
    for nm, M, K in BIGW:
        scr[nm] = nc.dram_tensor("s_" + nm, [M // 128, 128, K], BF16,
                                 kind="Internal").ap()
    scr["cv1"] = nc.dram_tensor("s_cv1", [DIC, 128, 4 * 128], BF16,
                                kind="Internal").ap()
    scr["cv2"] = nc.dram_tensor("s_cv2", [DHC, 128, 3 * 128], BF16,
                                kind="Internal").ap()
    s_h2 = nc.dram_tensor("s_h2", [DHC, 128, T], BF16, kind="Internal").ap()

    with ExitStack() as ctx:
        e = ctx.enter_context

        def sb(name, shape, dt=BF16):
            return e(nc.sbuf_tensor(name, shape, dt))

        x_tm = sb("x_tm", [128, NSUB * D], F32)[:]
        PA = sb("PA", [128, 2 * DIC * T])[:]
        xq_tm = PA[:, 0:(NSUB + 1) * D]
        u_sl = PA[:, 0:DIC * T]
        sz_sl = PA[:, DIC * T:2 * DIC * T]
        PB = sb("PB", [128, DHC * T])[:]
        xn_tm = PB[:, NS * T:NS * T + NSUB * D]
        Bb_sl = PB[:, 0:NS * T]
        Cb_sl = PB[:, NS * T:2 * NS * T]
        bc0 = PB[0:1, 0:2 * NS * T]
        h2_sl = PB
        PH = sb("PH", [128, 2 * DC * T])[:]
        xn_fm = PH[:, 0:DC * T]
        hfc1 = PH[:, DC * T:2 * DC * T]
        h_sl = PH[:, 0:NS * T]
        attnm = PH.bitcast(BF16)[:, 0:DC * T]
        PY = sb("PY", [128, DC * T], F32)[:]
        y_sl = PY.bitcast(BF16)[:, 0:DIC * T]
        attn32 = PY
        PF = sb("PF", [128, 4 * D], F32)[:]
        xn32 = PF[:, 0:D]
        xsq = PF[:, D:2 * D]
        qpre_tm = PF[:, 2 * D:3 * D]
        qi_tm = PF[:, 3 * D:4 * D].bitcast(I32)
        ffn32 = PF
        PC = sb("PC", [128, NSUB * D], F32)[:]
        a_ring = PC[:, 0:4 * T]
        dtr_r = PC[:, 4 * T:6 * T]
        esp_r = PC[:, 6 * T:8 * T]
        out_tm = PC
        h1r = sb("h1r", [128, 4 * (T + 2)])[:]
        h2r = sb("h2r", [128, 3 * T])[:]
        h1rv = h1r.rearrange("p (u s) -> p u s", s=T + 2)
        mx_t = sb("mx_t", [128, T])[:]
        mn_t = sb("mn_t", [128, T])[:]
        PD = sb("PD", [128, 8 * T])[:]
        b_ring = PD[:, 0:4 * T]
        dt_r = PD[:, 4 * T:6 * T]
        dtu_r = PD[:, 6 * T:8 * T]
        qpre_r = PD[:, 0:4 * T].bitcast(F32)
        qi_r = PD[:, 4 * T:8 * T].bitcast(I32)
        srow = PD[0:1, 0:5 * 128 * 2].bitcast(F32)
        srow_bf = PD[0:1, 5 * 128 * 2:5 * 128 * 3]
        xq_fm = sb("xq_fm", [128, DC * TX])[:]
        cacc = sb("cacc", [128, 2 * T], F32)[:]
        xdbl = cacc[0:96, 0:T]
        x_extra = cacc[0:1, 0:D]
        dtlow = cacc[:, T:T + T // 2].bitcast(BF16)[0:64, :]
        xc_ring = sb("xc_ring", [128, 2 * (T + 3)])[:]
        xc_tail = sb("xc_tail", [128, DIC * 3])[:]
        st_sl = sb("st_sl", [128, DIC * NS], F32)[:]
        h1_tail = sb("h1_tail", [128, DHC])[:]
        colA = sb("colA", [128, 8], F32)[:]
        colB = sb("colB", [128, 8], F32)[:]
        colC = sb("colC", [128, 8], F32)[:]
        m1c = sb("m1c", [128, NSUB + 1], F32)[:]
        s1c = sb("s1c", [128, NSUB + 1], F32)[:]
        f1c = sb("f1c", [128, NSUB + 1], F32)[:]
        m2c = sb("m2c", [128, NSUB], F32)[:]
        s2c = sb("s2c", [128, NSUB], F32)[:]
        f2c = sb("f2c", [128, NSUB], F32)[:]
        s2b = sb("s2b", [128, T])[:]
        f1b = sb("f1b", [128, T + 1])[:]
        f2b = sb("f2b", [128, T])[:]
        NSLOT = 2
        wring = sb("wring", [128, NSLOT * 4096])[:]
        A_sb = sb("A_sb", [128, DIC * NS], F32)[:]
        D_sb = sb("D_sb", [128, DIC], F32)[:]
        D_sbf = sb("D_sbf", [128, DIC])[:]
        cw1_sb = sb("cw1_sb", [128, DIC * 4], F32)[:]
        cb1_sb = sb("cb1_sb", [128, DIC], F32)[:]
        cw2_sb = sb("cw2_sb", [128, DHC * 3], F32)[:]
        cb2_sb = sb("cb2_sb", [128, DHC], F32)[:]
        fc1b_sb = sb("fc1b_sb", [128, DC], F32)[:]
        fc2b_sb = sb("fc2b_sb", [128, DC], F32)[:]
        dtb_sb = sb("dtb_sb", [128, DIC], F32)[:]
        xpT_sb = sb("xpT_sb", [128, DIC * 96])[:]
        dtpT_sb = e(nc.sbuf_tensor("dtpT_sb", [64, DI], BF16))[:]
        idnb = sb("idnb", [128, 128])[:]
        idnf = sb("idnf", [128, 128], F32)[:]
        onesb = sb("onesb", [1, 128])[:]
        onesf = sb("onesf", [1, 128], F32)[:]
        zeros = sb("zeros", [128, 256], F32)[:]
        zbf = zeros.bitcast(BF16)
        cst2 = sb("cst2", [128, 2], F32)[:]
        epsc = cst2[:, 0:1]
        onec = cst2[:, 1:2]
        cq1 = sb("cq1", [128, 1], F32)[:]
        cq2 = sb("cq2", [128, 1], F32)[:]
        swcol = sb("swcol", [128, 1], F32)[:]
        tcol = sb("tcol", [1, 4], F32)[:]
        pA = e(nc.psum_tensor("pA", [128, T], F32))
        pB = e(nc.psum_tensor("pB", [128, T], F32))
        pX = e(nc.psum_tensor("pX", [128, T], F32))
        pC = e(nc.psum_tensor("pC", [128, T], F32))
        pTb = e(nc.psum_tensor("pTb", [128, 2 * T], BF16))
        pFD = e(nc.psum_tensor("pFD", [128, 2 * T], F32))
        pF1 = pFD[:, 0:T]
        pF2 = pFD[:, T:2 * T]
        pS = e(nc.psum_tensor("pS", [128, T], F32))

        sems = {}
        for s in ["qw", "qx", "pe", "dv", "ac", "w0", "w1", "w2", "so", "tern",
                  "sh2", "shr"]:
            sems[s] = e(nc.semaphore(s))
        QW = Eng(sems["qw"], "qw")
        QX = Eng(sems["qx"], "qx")
        PE = Eng(sems["pe"], "pe", drain=DRAIN_MARKS)
        DV = Eng(sems["dv"], "dv", drain=DRAIN_MARKS)
        SC = Eng(sems["ac"], "ac", drain=DRAIN_MARKS)

        def ld(engq, dst, src):
            return engq.mark(lambda q, d=dst, s=src: q.dma_start(out=d, in_=s), 16)

        ld(QW, idnb[:], idn_bf)
        ld(QW, idnf[:], idn_f32)
        ld(QW, onesb[:], ones_row_bf)
        ld(QW, onesf[:], ones_row_f32)
        ld(QW, zeros, zeros_in)
        ld(QW, cst2, cst2_in)
        for dc in range(DIC):
            ld(QW, A_sb[:, dc * NS:(dc + 1) * NS], Alog[dc * 128:(dc + 1) * 128, :])
            ld(QW, cw1_sb[:, dc * 4:(dc + 1) * 4], cw1[dc * 128:(dc + 1) * 128, :])
        for hc in range(DHC):
            ld(QW, cw2_sb[:, hc * 3:(hc + 1) * 3], cw2[hc * 128:(hc + 1) * 128, :])

        def ld_rowvec(dst, src, nch):
            QX.mark(lambda q, d=dst, s=src, n=nch:
                    q.dma_start(out=d[:, 0:n],
                                in_=s.rearrange("one (c p) -> (one p) c",
                                                p=128),
                                allow_slow_non_contiguous=True), 16)

        ld_rowvec(D_sb, D_in, DIC)
        ld_rowvec(cb1_sb, cb1, DIC)
        ld_rowvec(cb2_sb, cb2, DHC)
        ld_rowvec(fc1b_sb, fc1b, DC)
        ld_rowvec(fc2b_sb, fc2b, DC)
        ld_rowvec(dtb_sb, dtb, DIC)
        for kc in range(DIC):
            QX.mark(lambda q, kc=kc: q.dma_start(
                out=xpT_sb[:, kc * 96:(kc + 1) * 96],
                in_=xpT[kc * 128:(kc + 1) * 128, :]), 16)
        QX.mark(lambda q: q.dma_start(out=dtpT_sb[:, :], in_=dtpT), 16)
        for nm in ("fc1", "inp", "op", "fc2"):
            src_ap, dst_ap = dram["h_" + nm], scr[nm]
            nblk = src_ap.shape[0]
            for b0 in range(nblk):
                QX.mark(lambda q, s=src_ap, d=dst_ap, b0=b0:
                        q.dma_start(out=d[b0], in_=s[b0]), 16)

        su_qw, su_qx = QW.n, QX.n
        QX.self_wait()
        QW.self_wait()
        for E2 in (DV, SC, PE):
            E2.wait(QW, su_qw)
            E2.wait(QX, su_qx)
        scm = SC.mark(lambda q: q.activation(
            A_sb[:, 0:DIC * NS], A_sb[:, 0:DIC * NS], AF.Exp))
        DV.wait(SC, scm)
        DV.mark(lambda q: q.tensor_copy(D_sbf[:], D_sb[:]))
        DV.mark(lambda q: q.tensor_scalar(A_sb[:, 0:DIC * NS], A_sb[:, 0:DIC * NS],
                                          -1.0, None, ALU.mult))

        PHASE[0] = "tern"
        def ternary_build(host_ap, scratch_ap, M, K, cq_col):
            tot = M * K
            ntile = tot // (128 * 4096)
            flat_src = host_ap.rearrange("a b c -> (a b c)") \
                              .rearrange("(r p c) -> r p c", p=128, c=4096)
            flat_dst = scratch_ap.rearrange("a b c -> (a b c)") \
                                 .rearrange("(r p c) -> r p c", p=128, c=4096)
            slot32 = PB.bitcast(F32)
            bfst = PH
            QX.wait(SC, SC.n)
            QX.wait(DV, DV.n)
            DV.mark(lambda q: q.tensor_copy(colA[:, 0:1], zeros[:, 0:1]))
            red_m = {}
            for p in range(ntile):
                sl = slot32[:, (p % 2) * 4096:(p % 2) * 4096 + 4096]
                if p >= 2:
                    QX.wait(DV, red_m[p - 2])
                qxm = QX.mark(lambda q, d=sl, s=flat_src, p=p:
                              q.dma_start(out=d, in_=s[p]), 16)
                QX.self_wait()
                DV.wait(QX, qxm)
                red_m[p] = DV.mark(lambda q, sl=sl: q.tensor_reduce(
                    colB[:, 0:1], sl, AX.X, ALU.add, apply_absolute_value=True))
                DV.mark(lambda q: q.tensor_tensor(
                    colA[:, 0:1], colA[:, 0:1], colB[:, 0:1], ALU.add))
            PE.wait(DV, DV.n)
            pem = PE.mark(lambda q: q.transpose(pS[0:1, 0:128], colA[:, 0:1], idnf[:]))
            DV.wait(PE, pem)
            DV.do(lambda q: q.tensor_reduce(tcol[0:1, 0:1], pS[0:1, 0:128],
                                            AX.X, ALU.add))
            DV.do(lambda q, n=float(tot): q.tensor_scalar(
                tcol[0:1, 0:1], tcol[0:1, 0:1], 1.0 / n, 1e-5, ALU.mult, ALU.max))
            dvm = DV.mark(lambda q: q.reciprocal(tcol[0:1, 1:2], tcol[0:1, 0:1]))
            PE.wait(DV, dvm)
            PE.do(lambda q: q.matmul(pS[:, 0:1], onesf[:], tcol[0:1, 1:2],
                                     start=True, stop=True))
            pem = PE.mark(lambda q: q.matmul(pS[:, 1:2], onesf[:], tcol[0:1, 0:1],
                                             start=True, stop=True))
            DV.wait(PE, pem)
            DV.do(lambda q: q.tensor_copy(swcol[:], pS[:, 0:1]))
            DV.mark(lambda q, c=cq_col: q.tensor_scalar(
                c, pS[:, 1:2], 1.0 / 127.0, None, ALU.mult))
            dv_mean = DV.n
            cast_m = {}
            tern_cnt = getattr(ternary_build, "_cnt", [0])
            ternary_build._cnt = tern_cnt
            for p in range(ntile):
                sl = slot32[:, (p % 2) * 4096:(p % 2) * 4096 + 4096]
                if p < 2:
                    QX.wait(DV, dv_mean)
                else:
                    QX.wait(DV, cast_m[p - 2][0])
                    QX.wait(SC, cast_m[p - 2][1])
                qxm = QX.mark(lambda q, d=sl, s=flat_src, p=p:
                              q.dma_start(out=d, in_=s[p]), 16)
                QX.self_wait()
                DV.wait(QX, qxm)
                DV.do(lambda q, sl=sl: q.tensor_scalar(
                    sl, sl, swcol[:, 0:1], 1.0, ALU.mult, ALU.min))
                DV.do(lambda q, sl=sl: q.tensor_scalar(
                    sl, sl, -1.0, None, ALU.max))
                dvm = DV.mark(lambda q, sl=sl: q.tensor_scalar(
                    sl, sl, 12582912.0, -12582912.0, ALU.add, ALU.add))
                SC.wait(DV, dvm)
                if tern_cnt[0] >= 2:
                    SC.wait_sem(sems["tern"], (tern_cnt[0] - 1) * 16)
                bb = bfst[:, (p % 2) * 4096:(p % 2) * 4096 + 4096]
                scm = SC.mark(lambda q, sl=sl, bb=bb: q.copy(bb, sl))
                cast_m[p] = (dvm, scm)
                QW.wait(SC, scm)
                QW.mark(lambda q, bb=bb, d=flat_dst, p=p:
                        q.dma_start(out=d[p], in_=bb), 16, sem=sems["tern"])
                tern_cnt[0] += 1
                QW.wait_sem(sems["tern"], tern_cnt[0] * 16)
            QX.wait(QW, QW.n)
            return DV.n

        ternary_build(dram["h_w1"], scr["w1"], DH, D, cq1)
        ternary_build(dram["h_w2"], scr["w2"], D, DH, cq2)
        tern_cnt = ternary_build._cnt
        diag_st = PH[:, 0:2 * 512]
        for which, nch, ntap, cw_sb in (("cv1", DIC, 4, cw1_sb),
                                        ("cv2", DHC, 3, cw2_sb)):
            for ch in range(nch):
                slot = ch % 2
                dbuf = diag_st[:, slot * 512: slot * 512 + ntap * 128]
                if tern_cnt[0] >= 2:
                    DV.wait_sem(sems["tern"], (tern_cnt[0] - 1) * 16)
                for tap in range(ntap):
                    fn = (lambda q, dbuf=dbuf, tap=tap, ch=ch, cw_sb=cw_sb,
                          ntap=ntap: q.tensor_scalar(
                              dbuf[:, tap * 128:(tap + 1) * 128], idnb[:],
                              cw_sb[:, ch * ntap + tap:ch * ntap + tap + 1],
                              None, ALU.mult))
                    if tap == ntap - 1:
                        dvm = DV.mark(fn)
                    else:
                        DV.do(fn)
                QW.wait(DV, dvm)
                QW.mark(lambda q, dbuf=dbuf, which=which, ch=ch:
                        q.dma_start(out=scr[which][ch], in_=dbuf), 16,
                        sem=sems["tern"])
                tern_cnt[0] += 1
                QW.wait_sem(sems["tern"], tern_cnt[0] * 16)
        for E2 in (DV, SC, PE, QX):
            E2.wait_sem(sems["tern"], ternary_build._cnt[0] * 16)

        pieces = []
        for mc in range(DC):
            pieces.append(("fc1", mc, D))
        for mc in range(DIC):
            pieces.append(("inp", mc, D))
            pieces.append(("cv1", mc, 4 * 128))
        for mc in range(DIC, 2 * DIC):
            pieces.append(("inp", mc, D))
        for mc in range(DHC):
            pieces.append(("w1", mc, D))
            pieces.append(("cv2", mc, 3 * 128))
        for mc in range(DC):
            pieces.append(("op", mc, DI))
        for mc in range(DC):
            pieces.append(("fc2", mc, D))
        for mc in range(DC):
            pieces.append(("w2", mc, DH))
        NPIECE = len(pieces)
        pr = {"idx": 0}
        piece_ready = []
        piece_free = []

        slot_cnt = [0] * NSLOT

        def issue_piece():
            i = pr["idx"]
            nm, mc, K = pieces[i % NPIECE]
            slot = i % NSLOT
            if i >= NSLOT:
                QW.wait(PE, piece_free[i - NSLOT])
            QW.mark(lambda q, nm=nm, mc=mc, K=K, slot=slot: q.dma_start(
                out=wring[:, slot * 4096: slot * 4096 + K],
                in_=scr[nm][mc]), 16, sem=sems["w%d" % slot])
            slot_cnt[slot] += 1
            piece_ready.append((slot, slot_cnt[slot] * 16))
            piece_free.append(None)
            pr["idx"] += 1

        def stat_ap(i, kc):
            slot = i % NSLOT
            return wring[:, slot * 4096 + kc * 128: slot * 4096 + (kc + 1) * 128]

        for _ in range(NSLOT):
            issue_piece()

        def mm_piece(pi, Kc, mov_fn, pp, extra=None):
            rslot, rcnt = piece_ready[pi]
            PE.wait_sem(sems["w%d" % rslot], rcnt)
            for kc in range(Kc):
                fn = (lambda q, o=pp, s=stat_ap(pi, kc), m=mov_fn(kc),
                      a=(kc == 0), z=(kc == Kc - 1):
                      q.matmul(o[:, 0:T], s, m, start=a, stop=z))
                if kc == Kc - 1 and extra is None:
                    pem = PE.mark(fn)
                else:
                    PE.do(fn)
            if extra is not None:
                po, mov2 = extra
                for kc in range(Kc):
                    fn = (lambda q, o=po, s=stat_ap(pi, kc), m=mov2(kc),
                          a=(kc == 0), z=(kc == Kc - 1):
                          q.matmul(o[:, 0:1], s, m, start=a, stop=z))
                    if kc == Kc - 1:
                        pem = PE.mark(fn)
                    else:
                        PE.do(fn)
            piece_free[pi] = pem
            return pem

        so_cnt = [0]
        sh2_cnt = [0]
        shr_cnt = [0]
        ffn_deq = []
        ffn_conv = []
        ffn_sil = []
        ffn_tree = []
        dbg_seen = set()

        def dbg(name, ap, tile):
            if not DEBUG or tile != 0 or name in dbg_seen:
                return
            dbg_seen.add(name)
            t = nc.dram_tensor("dbg_" + name, list(ap.shape), ap.dtype,
                               kind="ExternalOutput").ap()
            for E2 in (PE, DV, SC):
                QW.wait(E2, E2.n)
            QW.mark(lambda q, t=t, ap=ap: q.dma_start(out=t, in_=ap), 16,
                    sem=sems["so"])
            so_cnt[0] += 1
            QW.wait_sem(sems["so"], so_cnt[0] * 16)
            for E2 in (PE, DV, SC):
                E2.wait_sem(sems["so"], so_cnt[0] * 16)

        for tile in range(NTILES):
            b = tile // NTS
            ts = tile % NTS
            t0 = ts * T
            first = ts == 0
            last = ts == NTS - 1
            has_extra = not last
            NSUBX = NSUB + (1 if has_extra else 0)
            pbase = tile * NPIECE

            def need_piece(pi):
                while pr["idx"] <= pi:
                    issue_piece()

            QX.wait(DV, DV.n)
            DV.wait(PE, PE.n)
            SC.wait(PE, PE.n)
            DV.wait_sem(sems["so"], so_cnt[0] * 16)
            SC.wait_sem(sems["so"], so_cnt[0] * 16)

            PHASE[0] = "tm"
            xm = []
            for su in range(NSUB):
                qm = QX.mark(lambda q, b=b, t0=t0, su=su: q.dma_start(
                    out=x_tm[:, su * D:(su + 1) * D],
                    in_=x_in[b, t0 + su * 128: t0 + (su + 1) * 128, :]), 16)
                xm.append(qm)
            if has_extra:
                qm = QX.mark(lambda q, b=b, t0=t0: q.dma_start(
                    out=x_extra,
                    in_=x_in[b, t0 + T: t0 + T + 1, :]), 16)
                xm.append(qm)
            QX.self_wait()
            xm = [xm[-1]] * len(xm)

            dvm_lnprev = 0
            scm_lnprev = 0
            for su in range(NSUBX):
                r = 128 if su < NSUB else 1
                xs = x_tm[0:r, su * D:(su + 1) * D] if su < NSUB else x_extra
                DV.wait(QX, xm[su])
                DV.wait(SC, scm_lnprev)
                DV.do(lambda q, xs=xs, su=su, r=r: q.tensor_reduce(
                    colA[0:r, su:su + 1], xs, AX.X, ALU.add))
                SC.wait(QX, xm[su])
                SC.wait(DV, dvm_lnprev)
                scm = SC.mark(lambda q, xs=xs, su=su, r=r: q.activation(
                    xsq[0:r, 0:D], xs, AF.Square))
                DV.wait(SC, scm)
                DV.do(lambda q, su=su, r=r: q.tensor_reduce(
                    colB[0:r, su:su + 1], xsq[0:r, 0:D], AX.X, ALU.add))
                DV.do(lambda q, su=su, r=r: q.tensor_scalar(
                    colA[0:r, su:su + 1], colA[0:r, su:su + 1], 1.0 / D,
                    None, ALU.mult))
                DV.do(lambda q, su=su, r=r: q.tensor_scalar(
                    colB[0:r, su:su + 1], colB[0:r, su:su + 1], 1.0 / D,
                    None, ALU.mult))
                DV.do(lambda q, su=su, r=r: q.tensor_tensor(
                    colC[0:r, su:su + 1], colA[0:r, su:su + 1],
                    colA[0:r, su:su + 1], ALU.mult))
                dvm = DV.mark(lambda q, su=su, r=r: q.tensor_tensor(
                    colB[0:r, su:su + 1], colB[0:r, su:su + 1],
                    colC[0:r, su:su + 1], ALU.subtract))
                SC.wait(DV, dvm)
                SC.do(lambda q, su=su, r=r: q.activation(
                    colC[0:r, su:su + 1], colB[0:r, su:su + 1], AF.Ln,
                    bias=epsc[0:r, 0:1], scale=1.0))
                scm = SC.mark(lambda q, su=su, r=r: q.activation(
                    colC[0:r, su:su + 1], colC[0:r, su:su + 1], AF.Exp,
                    scale=-0.5))
                DV.wait(SC, scm)
                DV.do(lambda q, su=su, r=r: q.tensor_tensor(
                    colA[0:r, su:su + 1], colA[0:r, su:su + 1],
                    colC[0:r, su:su + 1], ALU.mult))
                dvm = DV.mark(lambda q, su=su, r=r: q.tensor_scalar(
                    colA[0:r, su:su + 1], colA[0:r, su:su + 1], -1.0,
                    None, ALU.mult))
                SC.wait(DV, dvm)
                scm = SC.mark(lambda q, xs=xs, su=su, r=r: q.activation(
                    xn32[0:r, 0:D], xs, AF.Identity,
                    bias=colA[0:r, su:su + 1], scale=colC[0:r, su:su + 1]))
                DV.wait(SC, scm)
                if su < NSUB:
                    DV.do(lambda q, su=su, r=r: q.tensor_copy(
                        xn_tm[0:r, su * D:(su + 1) * D], xn32[0:r, 0:D]))
                DV.do(lambda q, su=su, r=r: q.tensor_reduce(
                    m1c[0:r, su:su + 1], xn32[0:r, 0:D], AX.X, ALU.max,
                    apply_absolute_value=True))
                DV.do(lambda q, su=su, r=r: q.tensor_scalar(
                    m1c[0:r, su:su + 1], m1c[0:r, su:su + 1], 1e-5, None,
                    ALU.max))
                DV.do(lambda q, su=su, r=r: q.reciprocal(
                    s1c[0:r, su:su + 1], m1c[0:r, su:su + 1]))
                DV.do(lambda q, su=su, r=r: q.tensor_scalar(
                    s1c[0:r, su:su + 1], s1c[0:r, su:su + 1], 127.0, None,
                    ALU.mult))
                DV.do(lambda q, su=su, r=r: q.tensor_tensor(
                    f1c[0:r, su:su + 1], m1c[0:r, su:su + 1], cq1[0:r, 0:1],
                    ALU.mult))
                DV.do(lambda q, su=su, r=r: q.tensor_scalar(
                    qpre_tm[0:r, 0:D], xn32[0:r, 0:D], s1c[0:r, su:su + 1],
                    127.0, ALU.mult, ALU.min))
                DV.do(lambda q, r=r: q.tensor_scalar(
                    qpre_tm[0:r, 0:D], qpre_tm[0:r, 0:D], -128.0, None,
                    ALU.max))
                dvm = DV.mark(lambda q, r=r: q.tensor_scalar(
                    qpre_tm[0:r, 0:D], qpre_tm[0:r, 0:D], 12582912.0,
                    -12582912.0, ALU.add, ALU.add))
                SC.wait(DV, dvm)
                scm_lnprev = SC.mark(lambda q, su=su, r=r: q.copy(
                    xq_tm[0:r, su * D:(su + 1) * D], qpre_tm[0:r, 0:D]))
                dvm_lnprev = dvm
            sc_tm = SC.n
            dv_tm = DV.n
            dbg("x_tm", x_tm, tile)
            dbg("xn_tm", xn_tm, tile)
            dbg("xq_tm", xq_tm[:, 0:NSUB * D], tile)
            dbg("m1c", m1c[:, 0:NSUB], tile)

            PE.wait(SC, sc_tm)
            PE.wait(DV, dv_tm)
            for src_tm, dst_fm, stride, w_extra in (
                    (xn_tm, xn_fm, T, False), (xq_tm, xq_fm, TX, True)):
                for dc_i in range(DC):
                    xtr = w_extra and has_extra
                    for su in range(NSUB):
                        fn = (lambda q, s=src_tm, dc_i=dc_i, su=su: q.transpose(
                            pTb[:, su * 128:(su + 1) * 128],
                            s[:, su * D + dc_i * 128: su * D + (dc_i + 1) * 128],
                            idnb[:]))
                        if su == NSUB - 1 and not xtr:
                            pem = PE.mark(fn)
                        else:
                            PE.do(fn)
                    ncols = T
                    if xtr:
                        pem = PE.mark(lambda q, s=src_tm, dc_i=dc_i: q.transpose(
                            pTb[:, T:T + 1],
                            s[0:1, NSUB * D + dc_i * 128:
                              NSUB * D + (dc_i + 1) * 128], idnb[0:1, 0:1]))
                        ncols = T + 1
                    DV.wait(PE, pem)
                    dvm = DV.mark(lambda q, d=dst_fm, dc_i=dc_i, st=stride,
                                  nn=ncols: q.tensor_copy(
                                      d[:, dc_i * st:dc_i * st + nn],
                                      pTb[:, 0:nn]))
                    PE.wait(DV, dvm)
            if not has_extra:
                DV.mark(lambda q: q.tensor_copy(
                    xq_fm[:].rearrange("p (c t) -> p c t", t=TX)[:, :, T:T + 1],
                    zbf[:, 0:DC].unsqueeze(2)))

            dbg("xn_fm", xn_fm, tile)
            dbg("xq_fm", xq_fm, tile)
            PHASE[0] = "fm_mm"
            ev = {}
            ROT = (pA, pB, pF1, pF2)
            for mc in range(DC):
                pi = pbase + mc
                need_piece(pi)
                pp = ROT[mc % 4]
                if mc >= 4:
                    PE.wait(SC, ev[mc - 4])
                pem = mm_piece(pi, DC, lambda kc: xn_fm[:, kc * T:(kc + 1) * T],
                               pp)
                SC.wait(PE, pem)
                ev[mc] = SC.mark(lambda q, pp=pp, mc=mc: q.activation(
                    hfc1[:, mc * T:(mc + 1) * T], pp[:, 0:T], AF.Identity,
                    bias=fc1b_sb[:, mc:mc + 1], scale=1.0))
            PE.wait(SC, ev[DC - 1])

            silu_marks = {}
            ev = {}
            for mc in range(2 * DIC):
                if mc < DIC:
                    pi = pbase + DC + 2 * mc
                else:
                    pi = pbase + DC + DIC + mc
                need_piece(pi)
                pp = pA if mc % 2 == 0 else pB
                if mc >= 2:
                    PE.wait(SC, ev[mc - 2])
                pem = mm_piece(pi, DC, lambda kc: hfc1[:, kc * T:(kc + 1) * T],
                               pp)
                if mc >= DIC:
                    zc = mc - DIC
                    SC.wait(PE, pem)
                    ev[mc] = SC.mark(lambda q, pp=pp, zc=zc: q.activation(
                        sz_sl[:, zc * T:(zc + 1) * T], pp[:, 0:T], AF.Silu))
                else:
                    dc_i = mc
                    slot = dc_i % 2
                    xcb = xc_ring[:, slot * (T + 3): slot * (T + 3) + T + 3]
                    SC.wait(PE, pem)
                    scm = SC.mark(lambda q, pp=pp, xcb=xcb: q.activation(
                        xcb[:, 3:3 + T], pp[:, 0:T], AF.Identity))
                    ev[mc] = scm
                    DV.wait(SC, scm)
                    if first:
                        DV.do(lambda q, xcb=xcb: q.tensor_copy(
                            xcb[:, 0:3], zbf[:, 0:3]))
                    else:
                        DV.do(lambda q, xcb=xcb, dc_i=dc_i: q.tensor_copy(
                            xcb[:, 0:3], xc_tail[:, dc_i * 3:dc_i * 3 + 3]))
                    dvm = DV.mark(lambda q, xcb=xcb, dc_i=dc_i: q.tensor_copy(
                        xc_tail[:, dc_i * 3:dc_i * 3 + 3], xcb[:, T:T + 3]))
                    cpi = pbase + DC + 2 * mc + 1
                    need_piece(cpi)
                    pcv = pC if dc_i % 2 == 0 else pS
                    if dc_i >= 2:
                        PE.wait(SC, silu_marks[dc_i - 2])
                    PE.wait(DV, dvm)
                    rs, rc = piece_ready[cpi]
                    PE.wait_sem(sems["w%d" % rs], rc)
                    for tap in range(4):
                        fn = (lambda q, pcv=pcv, cpi=cpi, tap=tap, xcb=xcb:
                              q.matmul(pcv[:, 0:T], stat_ap(cpi, tap),
                                       xcb[:, tap:tap + T], start=(tap == 0),
                                       stop=(tap == 3)))
                        if tap == 3:
                            cpem = PE.mark(fn)
                        else:
                            PE.do(fn)
                    piece_free[cpi] = cpem
                    SC.wait(PE, cpem)
                    silu_marks[dc_i] = SC.mark(
                        lambda q, pcv=pcv, dc_i=dc_i: q.activation(
                            u_sl[:, dc_i * T:(dc_i + 1) * T], pcv[:, 0:T],
                            AF.Silu, bias=cb1_sb[:, dc_i:dc_i + 1], scale=1.0))
            PE.wait(SC, ev[2 * DIC - 1])
            PE.wait(SC, silu_marks[DIC - 1])
            PE.wait(SC, silu_marks[DIC - 2])
            sc_u = SC.n
            dbg("hfc1", hfc1, tile)
            dbg("u_sl", u_sl, tile)
            dbg("sz_sl", sz_sl, tile)

            PE.wait(SC, sc_u)
            for kc in range(DIC):
                fn = (lambda q, kc=kc, a=(kc == 0), z=(kc == DIC - 1):
                      q.matmul(pX[0:96, 0:T], xpT_sb[:, kc * 96:(kc + 1) * 96],
                               u_sl[:, kc * T:(kc + 1) * T], start=a, stop=z))
                if kc == DIC - 1:
                    pem = PE.mark(fn)
                else:
                    PE.do(fn)
            SC.wait(PE, pem)
            scm = SC.mark(lambda q: q.activation(xdbl[:, 0:T], pX[0:96, 0:T],
                                                 AF.Identity))
            DV.wait(SC, scm)
            dvm = DV.mark(lambda q: q.tensor_copy(dtlow[:, 0:T], xdbl[0:64, 0:T]))
            QX.wait(SC, scm)
            for row in range(32):
                qxm = QX.mark(lambda q, row=row: q.dma_start(
                    out=bc0[0:1, row * T:(row + 1) * T],
                    in_=xdbl[64 + row:65 + row, 0:T]), 16)
            QX.self_wait()
            PE.wait(DV, dvm)
            PE.wait(QX, qxm)

            for half, dstsl in ((0, Bb_sl), (1, Cb_sl)):
                for n in range(NS):
                    row = half * NS + n
                    pem = PE.mark(lambda q, row=row: q.matmul(
                        pC[:, 0:T], onesb[:],
                        bc0[0:1, row * T:(row + 1) * T],
                        start=True, stop=True))
                    DV.wait(PE, pem)
                    dvm = DV.mark(lambda q, d=dstsl, n=n: q.tensor_copy(
                        d[:, n * T:(n + 1) * T], pC[:, 0:T]))
                    PE.wait(DV, dvm)

            dbg("xdbl", xdbl, tile)
            dbg("Bb_sl", Bb_sl, tile)
            dbg("Cb_sl", Cb_sl, tile)

            PE.wait(DV, DV.n)
            for su in range(NSUBX):
                r = 128 if su < NSUB else 1
                pem = PE.mark(lambda q, su=su, r=r: q.transpose(
                    pS[0:1, 0:r], f1c[0:r, su:su + 1], idnf[0:r, 0:r]))
                DV.wait(PE, pem)
                dvm = DV.mark(lambda q, su=su, r=r: q.tensor_copy(
                    srow[0:1, su * 128:su * 128 + r], pS[0:1, 0:r]))
                PE.wait(DV, dvm)
            Tf = T + (1 if has_extra else 0)
            dvm = DV.mark(lambda q, Tf=Tf: q.tensor_copy(
                srow_bf[0:1, 0:Tf], srow[0:1, 0:Tf]))
            PE.wait(DV, dvm)
            pem = PE.mark(lambda q: q.matmul(pC[:, 0:T], onesb[:],
                                             srow_bf[0:1, 0:T],
                                             start=True, stop=True))
            DV.wait(PE, pem)
            dvm = DV.mark(lambda q: q.tensor_copy(f1b[:, 0:T], pC[:, 0:T]))
            PE.wait(DV, dvm)
            if has_extra:
                pem = PE.mark(lambda q: q.matmul(pC[:, 0:1], onesb[:],
                                                 srow_bf[0:1, T:T + 1],
                                                 start=True, stop=True))
                DV.wait(PE, pem)
                dvm = DV.mark(lambda q: q.tensor_copy(f1b[:, T:T + 1],
                                                      pC[:, 0:1]))
                PE.wait(DV, dvm)
            else:
                DV.mark(lambda q: q.tensor_copy(f1b[:, T:T + 1],
                                                zbf[:, 0:1]))
            PE.wait(DV, DV.n)

            w1_pend = []
            tree_pend = []

            def emit_w1(mc):
                g = len(ffn_deq) + len(w1_pend)
                pi = pbase + DC + 3 * DIC + 2 * mc
                need_piece(pi)
                pp = pF1 if mc % 2 == 0 else pF2
                px = pC[:, 0:1] if mc % 2 == 0 else pC[:, 1:2]
                if g >= 2:
                    PE.wait(DV, ffn_deq[g - 2])
                pem = mm_piece(
                    pi, DC, lambda kc: xq_fm[:, kc * TX:kc * TX + T], pp,
                    extra=(px, lambda kc: xq_fm[:, kc * TX + T:kc * TX + T + 1]))
                w1_pend.append((g, mc, pem, pp, px))

            def emit_rest():
                if not w1_pend:
                    return
                (gA, mcA, pemA, ppA, pxA), (gB, mcB, pemB, ppB, pxB) = w1_pend
                sl = gA % 4
                pair = h1rv[:, sl:sl + 2, :]
                DV.wait(PE, pemB)
                if gA >= 4:
                    DV.wait(PE, ffn_conv[gA - 3])
                DV.do(lambda q, pair=pair: q.tensor_tensor(
                    pair[:, :, 1:1 + T],
                    pFD[:].rearrange("p (u t) -> p u t", t=T),
                    f1b[:, 0:T].unsqueeze(1).broadcast_to([128, 2, T]),
                    ALU.mult))
                evd = DV.mark(lambda q, pair=pair: q.tensor_tensor(
                    pair[:, :, T + 1:T + 2],
                    pC[:, 0:2].unsqueeze(2),
                    f1b[:, T:T + 1].unsqueeze(1).broadcast_to([128, 2, 1]),
                    ALU.mult))
                ffn_deq.append(evd)
                ffn_deq.append(evd)
                if first:
                    DV.do(lambda q, pair=pair: q.tensor_copy(
                        pair[:, :, 0:1], zbf[:, 0:2].unsqueeze(2)))
                else:
                    DV.do(lambda q, pair=pair, mcA=mcA: q.tensor_copy(
                        pair[:, :, 0:1],
                        h1_tail[:, mcA:mcA + 2].unsqueeze(2)))
                dvm = DV.mark(lambda q, pair=pair, mcA=mcA: q.tensor_copy(
                    h1_tail[:, mcA:mcA + 2].unsqueeze(2),
                    pair[:, :, T:T + 1]))
                for g, mc, pem, pp, px in w1_pend:
                    h1b = h1r[:, (g % 4) * (T + 2):(g % 4) * (T + 2) + T + 2]
                    cpi = pbase + DC + 3 * DIC + 2 * mc + 1
                    need_piece(cpi)
                    pcv = pA if mc % 2 == 0 else pB
                    if g >= 2:
                        PE.wait(SC, ffn_sil[g - 2])
                    PE.wait(DV, dvm)
                    rs, rc = piece_ready[cpi]
                    PE.wait_sem(sems["w%d" % rs], rc)
                    for tap in range(3):
                        fn = (lambda q, pcv=pcv, cpi=cpi, tap=tap, h1b=h1b:
                              q.matmul(pcv[:, 0:T], stat_ap(cpi, tap),
                                       h1b[:, tap:tap + T], start=(tap == 0),
                                       stop=(tap == 2)))
                        if tap == 2:
                            cpem = PE.mark(fn)
                        else:
                            PE.do(fn)
                    piece_free[cpi] = cpem
                    ffn_conv.append(cpem)
                    h2b = h2r[:, (g % 3) * T:(g % 3 + 1) * T]
                    SC.wait(PE, cpem)
                    if g >= 3:
                        SC.wait_sem(sems["sh2"], (g - 2) * 16)
                    silm = SC.mark(
                        lambda q, pcv=pcv, h2b=h2b, mc=mc: q.activation(
                            h2b[:], pcv[:, 0:T], AF.Silu,
                            bias=cb2_sb[:, mc:mc + 1], scale=1.0))
                    ffn_sil.append(silm)
                    QX.wait(SC, silm)
                    if mc == 0:
                        QX.wait_sem(sems["shr"], shr_cnt[0] * 16)
                    QX.mark(lambda q, mc=mc, h2b=h2b: q.dma_start(
                        out=s_h2[mc], in_=h2b[:]), 16, sem=sems["sh2"])
                    sh2_cnt[0] += 1
                    QX.wait_sem(sems["sh2"], sh2_cnt[0] * 16)
                del w1_pend[:]

            PHASE[0] = "scan"
            if first:
                DV.do(lambda q: q.tensor_copy(st_sl[:], zeros[:]))
            dtu_marks = {}
            exp_marks = {}
            scan_hist = []
            b_grp = PD[:, 0:4 * T]
            for dc_i in range(DIC):
                par = dc_i % 2
                pem = PE.mark(lambda q, dc_i=dc_i: q.matmul(
                    pX[:, 0:T], dtpT_sb[:, dc_i * 128:(dc_i + 1) * 128],
                    dtlow[:, 0:T], start=True, stop=True))
                emit_rest()
                DV.wait(PE, pem)
                if dc_i >= 2:
                    DV.wait(SC, exp_marks[dc_i - 2])
                dtr = dtr_r[:, par * T:par * T + T]
                dvm = DV.mark(lambda q, dtr=dtr, dc_i=dc_i: q.tensor_scalar(
                    dtr, pX[:, 0:T], dtb_sb[:, dc_i:dc_i + 1], 80.0,
                    ALU.add, ALU.min))
                PE.wait(DV, dvm)
                SC.wait(DV, dvm)
                if dc_i >= 2:
                    SC.wait(DV, dtu_marks[dc_i - 2])
                esp = esp_r[:, par * T:par * T + T]
                dtc = dt_r[:, par * T:par * T + T]
                SC.do(lambda q, dtr=dtr, esp=esp: q.activation(esp, dtr, AF.Exp))
                exp_marks[dc_i] = SC.mark(lambda q, esp=esp, dtc=dtc: q.activation(
                    dtc, esp, AF.Ln, bias=onec[:, 0:1], scale=1.0))
                DV.wait(SC, exp_marks[dc_i])
                dtuc = dtu_r[:, par * T:par * T + T]
                dtu_marks[dc_i] = DV.mark(
                    lambda q, dtc=dtc, dtuc=dtuc, dc_i=dc_i: q.tensor_tensor(
                        dtuc, dtc, u_sl[:, dc_i * T:(dc_i + 1) * T], ALU.mult))
                for n in range(NS):
                    if n % 4 == 0:
                        g = n // 4
                        DV.mark(lambda q, dtuc=dtuc, g=g: q.tensor_tensor(
                            b_grp.rearrange("p (n t) -> p n t", t=T),
                            dtuc.unsqueeze(1).broadcast_to([128, 4, T]),
                            Bb_sl[:, g * 4 * T:(g + 1) * 4 * T]
                            .rearrange("p (n t) -> p n t", t=T), ALU.mult))
                    asl = n % 4
                    astr = a_ring[:, asl * T:asl * T + T]
                    if len(scan_hist) >= 4:
                        SC.wait(DV, scan_hist[-4])
                    scm = SC.mark(lambda q, astr=astr, dtc=dtc, dc_i=dc_i, n=n:
                                  q.activation(astr, dtc, AF.Exp,
                                               scale=A_sb[:, dc_i * NS + n:
                                                          dc_i * NS + n + 1]))
                    DV.wait(SC, scm)
                    init_ap = st_sl[:, dc_i * NS + n: dc_i * NS + n + 1]
                    scan_hist.append(DV.mark(
                        lambda q, astr=astr, asl=asl, n=n, i0=init_ap:
                        q.tensor_tensor_scan(h_sl[:, n * T:(n + 1) * T],
                                             astr,
                                             b_grp[:, asl * T:(asl + 1) * T],
                                             i0, ALU.mult, ALU.add)))
                DV.do(lambda q, dc_i=dc_i: q.tensor_copy(
                    st_sl[:, dc_i * NS:(dc_i + 1) * NS],
                    h_sl[:].rearrange("p (n t) -> p n t", t=T)[:, :, T - 1:T]
                    .rearrange("p n one -> p (n one)")))
                DV.do(lambda q: q.tensor_tensor(h_sl[:], h_sl[:], Cb_sl[:],
                                                ALU.mult))
                w = NS * T // 2
                while w > T:
                    DV.do(lambda q, w=w: q.tensor_tensor(
                        h_sl[:, 0:w], h_sl[:, 0:w], h_sl[:, w:2 * w], ALU.add))
                    w //= 2
                DV.mark(lambda q, dc_i=dc_i: q.tensor_tensor(
                    y_sl[:, dc_i * T:(dc_i + 1) * T], h_sl[:, 0:T],
                    h_sl[:, T:2 * T], ALU.add))
                emit_w1(2 * dc_i)
                emit_w1(2 * dc_i + 1)
            emit_rest()
            DV.do(lambda q: q.tensor_tensor(
                h_sl[:].rearrange("p (n t) -> p n t", t=T),
                u_sl.rearrange("p (c t) -> p c t", t=T),
                D_sbf.unsqueeze(2).broadcast_to([128, DIC, T]), ALU.mult))
            DV.do(lambda q: q.tensor_tensor(y_sl, y_sl, h_sl[:, 0:DIC * T],
                                            ALU.add))
            DV.mark(lambda q: q.tensor_tensor(y_sl, y_sl, sz_sl, ALU.mult))
            dv_y = DV.n
            dbg("y_sl", y_sl, tile)
            dbg("st_sl", st_sl, tile)

            PHASE[0] = "post"
            QX.wait(DV, dv_y)
            QX.wait_sem(sems["sh2"], sh2_cnt[0] * 16)
            QX.mark(lambda q: q.dma_start(
                out=h2_sl[:].rearrange("p (m t) -> p m t", t=T),
                in_=s_h2.rearrange("m p t -> p m t")),
                16, sem=sems["shr"])
            shr_cnt[0] += 1

            PE.wait(DV, dv_y)
            ev = {}
            for mc in range(DC):
                pi = pbase + DC + 3 * DIC + 2 * DHC + mc
                need_piece(pi)
                pp = ROT[mc % 4]
                if mc >= 4:
                    PE.wait(SC, ev[mc - 4])
                pem = mm_piece(pi, DIC, lambda kc: y_sl[:, kc * T:(kc + 1) * T],
                               pp)
                SC.wait(PE, pem)
                ev[mc] = SC.mark(lambda q, pp=pp, mc=mc: q.activation(
                    attnm[:, mc * T:(mc + 1) * T], pp[:, 0:T], AF.Identity))
            PE.wait(SC, ev[DC - 1])

            ev = {}
            for mc in range(DC):
                pi = pbase + DC + 3 * DIC + 2 * DHC + DC + mc
                need_piece(pi)
                pp = ROT[mc % 4]
                if mc >= 4:
                    PE.wait(SC, ev[mc - 4])
                pem = mm_piece(pi, DC, lambda kc: attnm[:, kc * T:(kc + 1) * T],
                               pp)
                SC.wait(PE, pem)
                ev[mc] = SC.mark(lambda q, pp=pp, mc=mc: q.activation(
                    attn32[:, mc * T:(mc + 1) * T], pp[:, 0:T], AF.Identity,
                    bias=fc2b_sb[:, mc:mc + 1], scale=1.0))
            PE.wait(SC, ev[DC - 1])

            dbg("attn32", attn32, tile)
            PHASE[0] = "ffn"
            dbg("f1b", f1b, tile)
            DV.wait_sem(sems["shr"], shr_cnt[0] * 16)
            tscr = PF.bitcast(BF16)
            for dst, op in ((mx_t, ALU.max), (mn_t, ALU.min)):
                DV.do(lambda q, op=op: q.tensor_tensor(
                    tscr[:, 0:16 * T], h2_sl[:, 0:16 * T],
                    h2_sl[:, 16 * T:32 * T], op))
                w = 8 * T
                while w > T:
                    DV.do(lambda q, w=w, op=op: q.tensor_tensor(
                        tscr[:, 0:w], tscr[:, 0:w], tscr[:, w:2 * w], op))
                    w //= 2
                DV.mark(lambda q, dst=dst, op=op: q.tensor_tensor(
                    dst[:], tscr[:, 0:T], tscr[:, T:2 * T], op))
            DV.do(lambda q: q.tensor_scalar(mn_t[:, 0:T], mn_t[:, 0:T], -1.0,
                                            None, ALU.mult))
            dvm = DV.mark(lambda q: q.tensor_tensor(mx_t[:, 0:T], mx_t[:, 0:T],
                                                    mn_t[:, 0:T], ALU.max))
            PE.wait(DV, dvm)
            for su in range(NSUB):
                pem = PE.mark(lambda q, su=su: q.transpose(
                    pTb[:, 0:128], mx_t[:, su * 128:(su + 1) * 128], idnb[:]))
                DV.wait(PE, pem)
                dvm = DV.mark(lambda q, su=su: q.tensor_reduce(
                    m2c[:, su:su + 1], pTb[:, 0:128], AX.X, ALU.max))
                PE.wait(DV, dvm)
            DV.do(lambda q: q.tensor_scalar(m2c[:, 0:NSUB], m2c[:, 0:NSUB],
                                            1e-5, None, ALU.max))
            DV.do(lambda q: q.reciprocal(s2c[:, 0:NSUB], m2c[:, 0:NSUB]))
            DV.do(lambda q: q.tensor_scalar(s2c[:, 0:NSUB], s2c[:, 0:NSUB],
                                            127.0, None, ALU.mult))
            dvm = DV.mark(lambda q: q.tensor_tensor(
                f2c[:, 0:NSUB], m2c[:, 0:NSUB],
                cq2[:, 0:1].broadcast_to([128, NSUB]), ALU.mult))
            PE.wait(DV, dvm)
            for colsrc, dstb in ((s2c, s2b), (f2c, f2b)):
                for su in range(NSUB):
                    pem = PE.mark(lambda q, c=colsrc, su=su: q.transpose(
                        pS[0:1, 0:128], c[:, su:su + 1], idnf[:]))
                    DV.wait(PE, pem)
                    dvm = DV.mark(lambda q, su=su: q.tensor_copy(
                        srow[0:1, su * 128:(su + 1) * 128], pS[0:1, 0:128]))
                    PE.wait(DV, dvm)
                dvm = DV.mark(lambda q: q.tensor_copy(srow_bf[0:1, 0:T],
                                                      srow[0:1, 0:T]))
                PE.wait(DV, dvm)
                pem = PE.mark(lambda q: q.matmul(pC[:, 0:T], onesb[:],
                                                 srow_bf[0:1, 0:T],
                                                 start=True, stop=True))
                DV.wait(PE, pem)
                dvm = DV.mark(lambda q, d=dstb: q.tensor_copy(d[:, 0:T],
                                                              pC[:, 0:T]))
                PE.wait(DV, dvm)

            DV.wait_sem(sems["shr"], shr_cnt[0] * 16)
            cast_marks = {}
            qp_full = PD.bitcast(F32)
            NB = 4 * T
            for blk in range(DHC * T // NB):
                qp = qp_full[:, 0:NB] if False else qp_full
                if blk >= 1:
                    DV.wait(SC, cast_marks[blk - 1])
                DV.do(lambda q, blk=blk: q.tensor_tensor(
                    qp_full.rearrange("p (c t) -> p c t", t=T),
                    h2_sl[:, blk * NB:(blk + 1) * NB]
                    .rearrange("p (c t) -> p c t", t=T),
                    s2b.unsqueeze(1).broadcast_to([128, 4, T]), ALU.mult))
                DV.do(lambda q: q.tensor_scalar(
                    qp_full, qp_full, 127.0, -128.0, ALU.min, ALU.max))
                dvm = DV.mark(lambda q: q.tensor_scalar(
                    qp_full, qp_full, 12582912.0, -12582912.0, ALU.add,
                    ALU.add))
                SC.wait(DV, dvm)
                cast_marks[blk] = SC.mark(lambda q, blk=blk: q.copy(
                    h2_sl[:, blk * NB:(blk + 1) * NB], qp_full))
            sc_q2 = SC.n

            dbg("q2b", h2_sl, tile)
            dbg("m2c", m2c, tile)
            dbg("s2b", s2b, tile)
            dbg("f2b", f2b, tile)
            PE.wait(SC, sc_q2)
            ev = {}
            for mc in range(DC):
                pi = pbase + DC + 3 * DIC + 2 * DC + 2 * DHC + mc
                need_piece(pi)
                pp = ROT[mc % 4]
                if mc >= 4:
                    PE.wait(DV, ev[mc - 4])
                pem = mm_piece(pi, DHC,
                               lambda kc: h2_sl[:, kc * T:(kc + 1) * T], pp)
                DV.wait(PE, pem)
                ev[mc] = DV.mark(lambda q, pp=pp, mc=mc: q.tensor_tensor(
                    ffn32[:, mc * T:(mc + 1) * T], pp[:, 0:T], f2b[:, 0:T],
                    ALU.mult))
            PE.wait(DV, ev[DC - 1])

            dbg("ffn32", ffn32, tile)
            PHASE[0] = "out"
            for su in range(NSUB):
                for half in range(2):
                    for j in range(4):
                        dci = half * 4 + j
                        fn = (lambda q, su=su, dci=dci, j=j: q.transpose(
                            pS[:, j * 128:(j + 1) * 128],
                            attn32[:, dci * T + su * 128:
                                   dci * T + (su + 1) * 128], idnf[:]))
                        if j == 3:
                            pem = PE.mark(fn)
                        else:
                            PE.do(fn)
                    DV.wait(PE, pem)
                    off = su * D + half * 512
                    dvm = DV.mark(lambda q, off=off: q.scalar_tensor_tensor(
                        out_tm[:, off:off + 512], pS[:, 0:512], onec[:, 0:1],
                        x_tm[:, off:off + 512], ALU.mult, ALU.add))
                    PE.wait(DV, dvm)
                    for j in range(4):
                        dci = half * 4 + j
                        fn = (lambda q, su=su, dci=dci, j=j: q.transpose(
                            pS[:, j * 128:(j + 1) * 128],
                            ffn32[:, dci * T + su * 128:
                                  dci * T + (su + 1) * 128], idnf[:]))
                        if j == 3:
                            pem = PE.mark(fn)
                        else:
                            PE.do(fn)
                    DV.wait(PE, pem)
                    dvm = DV.mark(lambda q, off=off: q.tensor_tensor(
                        out_tm[:, off:off + 512], out_tm[:, off:off + 512],
                        pS[:, 0:512], ALU.add))
                    PE.wait(DV, dvm)
                QW.wait(DV, DV.n)
                QW.mark(lambda q, b=b, t0=t0, su=su: q.dma_start(
                    out=out_d[b, t0 + su * 128: t0 + (su + 1) * 128, :],
                    in_=out_tm[:, su * D:(su + 1) * D]), 16, sem=sems["so"])
                so_cnt[0] += 1
            QW.wait_sem(sems["so"], so_cnt[0] * 16)

        QW.do(lambda q, c=None: q.wait_ge(sems["so"], so_cnt[0] * 16))

        block = e(nc.Block())

        @block.sync
        def _(q):
            for fn in QW.ops:
                fn(q)

        @block.gpsimd
        def _(q):
            for fn in QX.ops:
                fn(q)

        @block.tensor
        def _(q):
            for fn in PE.ops:
                fn(q)

        @block.vector
        def _(q):
            for fn in DV.ops:
                fn(q)

        @block.scalar
        def _(q):
            for fn in SC.ops:
                fn(q)

    return nc


def _blocked(w):
    M, K = w.shape
    Mc, Kc = M // 128, K // 128
    v = w.reshape(Mc, 128, Kc, 128).transpose(0, 3, 2, 1)
    return np.ascontiguousarray(v.reshape(Mc, 128, K))


def kernel(L0_attn_x, L0_attn_fc1_w, L0_attn_fc1_b, L0_attn_in_proj_w,
           L0_attn_conv_w, L0_attn_conv_b, L0_attn_x_proj_w, L0_attn_dt_proj_w,
           L0_attn_dt_proj_b, L0_attn_A_log, L0_attn_D, L0_attn_out_proj_w,
           L0_attn_fc2_w, L0_attn_fc2_b, L0_ffn_w1, L0_ffn_conv_w,
           L0_ffn_conv_b, L0_ffn_w2):
    f = np.float32
    c = np.ascontiguousarray
    x = np.asarray(L0_attn_x, f)

    common = {
        "h_fc1": _blocked(np.asarray(L0_attn_fc1_w, f)),
        "h_inp": _blocked(np.asarray(L0_attn_in_proj_w, f)),
        "h_op": _blocked(np.asarray(L0_attn_out_proj_w, f)),
        "h_fc2": _blocked(np.asarray(L0_attn_fc2_w, f)),
        "h_w1": _blocked(np.asarray(L0_ffn_w1, f)),
        "h_w2": _blocked(np.asarray(L0_ffn_w2, f)),
        "fc1b": c(np.asarray(L0_attn_fc1_b, f)[None, :]),
        "fc2b": c(np.asarray(L0_attn_fc2_b, f)[None, :]),
        "cw1": c(np.asarray(L0_attn_conv_w, f)),
        "cb1": c(np.asarray(L0_attn_conv_b, f)[None, :]),
        "xpT": c(np.asarray(L0_attn_x_proj_w, f).T),
        "dtpT": c(np.asarray(L0_attn_dt_proj_w, f).T),
        "dtb": c(np.asarray(L0_attn_dt_proj_b, f)[None, :]),
        "Alog": c(np.asarray(L0_attn_A_log, f)),
        "Dvec": c(np.asarray(L0_attn_D, f)[None, :]),
        "cw2": c(np.asarray(L0_ffn_conv_w, f)[:, 0, :]),
        "cb2": c(np.asarray(L0_ffn_conv_b, f)[None, :]),
        "idn_bf": np.eye(128, dtype=NBF),
        "idn_f32": np.eye(128, dtype=f),
        "ones_row_bf": np.ones((1, 128), NBF),
        "ones_row_f32": np.ones((1, 128), f),
        "zeros_in": np.zeros((128, 256), f),
        "cst2_in": c(np.stack([np.full(128, 1e-5, f),
                               np.ones(128, f)], axis=1)),
    }

    nc = build_program()
    in_maps = []
    for core in range(NCORES):
        m = dict(common)
        m["x"] = c(x[core * BC:(core + 1) * BC])
        in_maps.append(m)
    res = run_bass_kernel_spmd(nc, in_maps, core_ids=list(range(NCORES)))
    return np.concatenate([res.results[cc]["out"] for cc in range(NCORES)],
                          axis=0)
